# version 12
# baseline (speedup 1.0000x reference)
import numpy as np
import ml_dtypes
import concourse.bass as bass
import concourse.mybir as mybir
from concourse.bass_utils import run_bass_kernel_spmd

BFnp = ml_dtypes.bfloat16
F32 = mybir.dt.float32
BF16 = mybir.dt.bfloat16
AF = mybir.ActivationFunctionType
ALU = mybir.AluOpType
AX = mybir.AxisListType

B, T, C = 2, 8192, 2048
NH, L = 16, 128
NE = 32
TOK = 2048
NCT = C // 128
NEG = -1.0e30


def build(debug=False, upto="all"):
    nc = bass.Bass("TRN2", target_bir_lowering=False)

    x_full = nc.dram_tensor("x_full", [T, C], F32, kind="ExternalInput")
    x_own = nc.dram_tensor("x_own", [TOK, C], F32, kind="ExternalInput")
    masks_in = nc.dram_tensor("masks_in", [16, 128, 512], BF16, kind="ExternalInput")
    ident_in = nc.dram_tensor("ident_in", [128, 128], BF16, kind="ExternalInput")
    identf_in = nc.dram_tensor("identf_in", [128, 128], F32, kind="ExternalInput")
    wlat_in = nc.dram_tensor("wlat_in", [128, NCT, 128], F32, kind="ExternalInput")
    blat_in = nc.dram_tensor("blat_in", [L], F32, kind="ExternalInput")
    wd_in = nc.dram_tensor("wd_in", [NH, 128, NCT, 128], F32, kind="ExternalInput")
    bd_in = nc.dram_tensor("bd_in", [C], F32, kind="ExternalInput")
    wproj_in = nc.dram_tensor("wproj_in", [4, 128, NCT, 512], F32, kind="ExternalInput")
    bproj_in = nc.dram_tensor("bproj_in", [C], F32, kind="ExternalInput")
    gate_in = nc.dram_tensor("gate_in", [128, NCT, 32], F32, kind="ExternalInput")
    wgu_in = nc.dram_tensor("wgu_in", [NE, 12, 128, NCT, 128], F32, kind="ExternalInput")
    wdn_in = nc.dram_tensor("wdn_in", [NE, 4, 6, 128, 512], F32, kind="ExternalInput")

    out_own = nc.dram_tensor("out_own", [TOK, C], F32, kind="ExternalOutput")
    skind = "ExternalOutput" if debug else "Internal"
    r1_dram = nc.dram_tensor("r1_dram", [TOK, C], F32, kind=skind)
    wproj_bf_dram = nc.dram_tensor("wproj_bf_dram", [4, 128, NCT, 512], BF16,
                                   kind="Internal")
    xn1t_dram = nc.dram_tensor("xn1t_dram", [16, 128, 2048], BF16, kind=skind)
    if debug:
        dbg_kt = nc.dram_tensor("dbg_kt", [128, T], BF16, kind="ExternalOutput")
        dbg_knat = nc.dram_tensor("dbg_knat", [128, T], BF16, kind="ExternalOutput")
        dbg_qt = nc.dram_tensor("dbg_qt", [128, NH * TOK], BF16, kind="ExternalOutput")
        dbg_comb = nc.dram_tensor("dbg_comb", [128, 512], F32, kind="ExternalOutput")

    SB_BASE = 17408
    LIMIT = 229120
    cur = [SB_BASE]

    def sba(name, shape, dt, at=None):
        esz = {F32: 4, BF16: 2}[dt]
        nbytes = int(np.prod(shape[1:])) * esz
        if at is None:
            off = (cur[0] + 63) // 64 * 64
            cur[0] = off + nbytes
            assert cur[0] <= LIMIT, (name, cur[0])
        else:
            off = at
            assert off + nbytes <= LIMIT, (name, off)
        return nc.alloc_sbuf_tensor_at(name, list(shape), dt, offset=off)

    ident = sba("ident", [128, 128], BF16)
    identf = sba("identf", [128, 128], F32)
    onescol = sba("onescol", [128, 1], BF16)
    onesrow = sba("onesrow", [1, 128], BF16)
    c_eps = sba("c_eps", [128, 1], F32)
    c_nh = sba("c_nh", [128, 1], F32)
    c_n1 = sba("c_n1", [128, 1], F32)
    c_isq = sba("c_isq", [128, 1], F32)
    blat_col = sba("blat_col", [128, 1], F32)
    bd_cols = sba("bd_cols", [128, NH], F32)
    bd_sc = sba("bd_sc", [128, NH], F32)
    bproj_row = sba("bproj_row", [1, C], BF16)
    bpb_bf = sba("bpb_bf", [128, C], BF16)
    gate_bf = sba("gate_bf", [128, NCT * 32], BF16)
    comb = sba("comb", [128, 16 * 32], F32)
    mu = sba("mu", [128, 1], F32)
    ssq = sba("ssq", [128, 1], F32)
    vvar = sba("vvar", [128, 1], F32)
    sinv = sba("sinv", [128, 1], F32)
    musin = sba("musin", [128, 1], F32)
    tcol = sba("tcol", [128, 1], F32)
    t2col = sba("t2col", [128, 1], F32)
    lncol = sba("lncol", [128, 1], F32)
    mu_b = sba("mu_b", [128, 1], F32)
    ssq_b = sba("ssq_b", [128, 1], F32)
    vvar_b = sba("vvar_b", [128, 1], F32)
    sinv_b = sba("sinv_b", [128, 1], F32)
    musin_b = sba("musin_b", [128, 1], F32)
    tcol_b = sba("tcol_b", [128, 1], F32)
    t2col_b = sba("t2col_b", [128, 1], F32)
    lncol_b = sba("lncol_b", [128, 1], F32)
    stcol = sba("stcol", [128, 1], F32)
    stcol_b = sba("stcol_b", [128, 1], F32)
    gden = sba("gden", [128, 1], F32)
    gdeni = sba("gdeni", [128, 1], F32)
    gthr = sba("gthr", [128, 1], F32)
    gSk = sba("gSk", [128, 1], F32)
    gSki = sba("gSki", [128, 1], F32)
    gtau = sba("gtau", [128, 1], F32)
    gmc = sba("gmc", [128, 1], F32)
    kmean = sba("kmean", [128, 1], F32)
    probs = sba("probs", [128, 32], F32)
    gwork = sba("gwork", [128, 32], F32)
    gtmp = sba("gtmp", [128, 32], F32)
    gm8 = sba("gm8", [128, 8], F32)
    gcs = sba("gcs", [128, 8], F32)
    gkeep = sba("gkeep", [128, 8], F32)
    PERS_END = cur[0]

    kc_sb = sba("kc_sb", [128, T], BF16)
    knat = sba("knat", [128, T], BF16)
    qT = sba("qT", [128, NH * TOK], BF16)
    wlat_bf = sba("wlat_bf", [128, NCT * 128], BF16)
    WLAT_OFF = cur[0] - 2 * NCT * 128
    slat_neg = sba("slat_neg", [1, 128], BF16)
    dinv_f = sba("dinv_f", [1, 512], F32)
    dinv_bf = sba("dinv_bf", [1, 512], BF16)
    ATTN_FIX_END = cur[0]

    kT_sb = sba("kT_sb", [128, T], BF16)
    xt0 = sba("xt0", [128, C], F32)
    xt1 = sba("xt1", [128, C], F32)
    za_bf = sba("za_bf", [128, C], BF16)
    za_bf_b = sba("za_bf_b", [128, C], BF16)
    zaT = sba("zaT", [128, NCT * 512], BF16)
    sq_scr = sba("sq_scr", [128, C], BF16)
    wdh_bf0 = sba("wdh_bf0", [128, NCT * 128], BF16)
    wdh_bf1 = sba("wdh_bf1", [128, NCT * 128], BF16)
    wpc0 = sba("wpc0", [128, 1024], BF16)
    wpc1 = sba("wpc1", [128, 1024], BF16)
    xt2 = sba("xt2", [128, C], F32)
    mrow0 = sba("mrow0", [1, 512], BF16)
    mrow1 = sba("mrow1", [1, 512], BF16)
    sd_neg = sba("sd_neg", [1, C], BF16)
    A0_END = cur[0]

    cur[0] = ATTN_FIX_END
    masks_sb = sba("masks_sb", [128, 16 * 512], BF16)
    xq_sb = sba("xq_sb", [128, C], F32)
    XQ_OFF = (cur[0] - 4 * C)
    wp_bf0 = sba("wp_bf0", [128, NCT * 512], BF16)
    wp_bf1 = sba("wp_bf1", [128, NCT * 512], BF16)
    yT_all = sba("yT_all", [128, NH * 512], BF16)
    pT0 = sba("pT0", [128, 512], BF16)
    pT1 = sba("pT1", [128, 512], BF16)
    pT2 = sba("pT2", [128, 512], BF16)
    denb = sba("denb", [128, 512], BF16)
    r1_tile = sba("r1_tile", [128, C], F32)
    xn1_nat = sba("xn1_nat", [128, C], BF16)
    xn1_st = sba("xn1_st", [128, C], BF16, at=WLAT_OFF)
    AM_END = cur[0]

    cur[0] = PERS_END
    xn1T_sb = sba("xn1T_sb", [128, NCT * 1024], BF16)
    acc = sba("acc", [128, 8 * C], F32)
    wgu_bf0 = sba("wgu_bf0", [128, NCT * 128], BF16)
    wgu_bf1 = sba("wgu_bf1", [128, NCT * 128], BF16)
    wgu_bf2 = sba("wgu_bf2", [128, NCT * 128], BF16)
    wdn_bf = sba("wdn_bf", [128, 4 * 6 * 512], BF16)
    g_sil = sba("g_sil", [128, 6 * 1024], BF16)
    hT0 = sba("hT0", [128, 6 * 1024], BF16)
    hT1 = sba("hT1", [128, 6 * 1024], BF16)
    r1rd0 = sba("r1rd0", [128, C], F32)
    r1rd1 = sba("r1rd1", [128, C], F32)
    FFN_END = cur[0]

    PS_A = nc.alloc_psum_tensor("PS_A", [128, 512], F32)
    PS_B = nc.alloc_psum_tensor("PS_B", [128, 512], F32)
    PS_G = nc.alloc_psum_tensor("PS_G", [128, 512], F32)
    PS_C = nc.alloc_psum_tensor("PS_C", [128, 512], F32)
    PS_D = nc.alloc_psum_tensor("PS_D", [128, 512], F32)
    PS_F = nc.alloc_psum_tensor("PS_F", [128, 512], F32)
    PS_E = nc.alloc_psum_tensor("PS_E", [128, 2048], BF16)
    PS_AB = [PS_A, PS_B]
    PS_CD = [PS_C, PS_D]
    PS_R3 = [PS_A, PS_B, PS_G]

    bank = {k: {"act": 0, "dve": 0} for k in ["A", "B", "G", "C", "D", "F", "E0", "E1"]}
    R3N = ["A", "B", "G"]
    ABN = ["A", "B"]
    CDN = ["C", "D"]

    sems = {}
    cnt = {}
    for name in ["pe", "act", "dve", "d_xt0", "d_xt1", "d_xt2", "d_wd0", "d_wd1",
                 "d_wp0", "d_wp1", "d_wgu0", "d_wgu1", "d_wgu2", "d_xn1t",
                 "d_wdn", "d_scr_r1", "d_scr_xn", "d_wpc0", "d_wpc1",
                 "d_scrw0", "d_scrw1",
                 "d_r10", "d_r11", "d_xq", "d_scrw", "d_setup", "d_setupg",
                 "d_out", "d_out0", "d_out1", "d_mask"]:
        sems[name] = nc.alloc_semaphore(name="s_" + name)
        cnt[name] = 0

    def inc(instr, name, k=1):
        instr.then_inc(sems[name], k)
        cnt[name] += k
        return cnt[name]

    def wt(eng, name, val):
        if val > 0:
            eng.wait_ge(sems[name], val)

    def wbank(eng, bn):
        wt(eng, "act", bank[bn]["act"])
        wt(eng, "dve", bank[bn]["dve"])

    class ChainEng:

        def __init__(self, eng, name):
            self._eng = eng
            self._name = name

        def wait_ge(self, sem, val):
            return self._eng.wait_ge(sem, val)

        def __getattr__(self, attr):
            fn = getattr(self._eng, attr)

            def call(*a, **k):
                self._eng.wait_ge(sems[self._name], cnt[self._name])
                ins = fn(*a, **k)
                ins.then_inc(sems[self._name], 1)
                cnt[self._name] += 1
                return ins

            return call

    PE, SP, GP = nc.tensor, nc.sync, nc.gpsimd
    ACTE = ChainEng(nc.scalar, "act")
    DVE = ChainEng(nc.vector, "dve")

    with nc.Block() as blk:
        for dst, src in [(ident, ident_in.ap()), (identf, identf_in.ap())]:
            SP.dma_start(dst[:], src).then_inc(sems["d_setup"], 16)
            cnt["d_setup"] += 16
        SP.dma_start(blat_col[:], blat_in.ap().rearrange("(p o) -> p o", o=1),
                     allow_slow_non_contiguous=True).then_inc(sems["d_setup"], 16)
        cnt["d_setup"] += 16
        SP.dma_start(bd_cols[:], bd_in.ap().rearrange("(h p) -> p h", p=128),
                     allow_slow_non_contiguous=True).then_inc(sems["d_setup"], 16)
        cnt["d_setup"] += 16
        GP.dma_start(wlat_bf[:], wlat_in.ap().rearrange("p c l -> p (c l)")
                     ).then_inc(sems["d_setupg"], 16)
        cnt["d_setupg"] += 16
        GP.dma_start(gate_bf[:], gate_in.ap().rearrange("p c e -> p (c e)")
                     ).then_inc(sems["d_setupg"], 16)
        cnt["d_setupg"] += 16
        GP.dma_start(bproj_row[:], bproj_in.ap().rearrange("(o c) -> o c", o=1)
                     ).then_inc(sems["d_setupg"], 16)
        cnt["d_setupg"] += 16
        setup_dma = cnt["d_setup"]
        setup_gp = cnt["d_setupg"]

        DVE.memset(onescol[:], 1.0)
        DVE.memset(onesrow[:], 1.0)
        DVE.memset(c_eps[:], 1e-5)
        DVE.memset(c_nh[:], -0.5)
        DVE.memset(c_n1[:], -1.0)
        DVE.memset(c_isq[:], 1.0 / float(np.sqrt(128.0)))
        wt(DVE, "d_setup", setup_dma)
        wt(DVE, "d_setupg", setup_gp)
        DVE.tensor_single_scalar(
            bd_sc[:], bd_cols[:], 1.0 / float(np.sqrt(128.0)), ALU.mult)
        v_setup = cnt["dve"]

        wt(PE, "d_setup", setup_dma)
        wt(PE, "d_setupg", setup_gp)
        wt(PE, "dve", v_setup)
        for ct in range(NCT):
            li = PE.matmul(PS_F[0:1, 0:128], onescol[:],
                           wlat_bf[:, ct * 128:(ct + 1) * 128],
                           start=(ct == 0), stop=(ct == NCT - 1))
        p_slat = inc(li, "pe")
        wt(ACTE, "pe", p_slat)
        ACTE.activation(slat_neg[:], PS_F[0:1, 0:128], AF.Identity,
                                     scale=c_n1[0:1, :])
        a_slat = cnt["act"]
        bank["F"]["act"] = a_slat
        v_bpb = 0
        for cs in range(4):
            wbank(PE, "G")
            li = PE.matmul(PS_G[:, :], onesrow[:],
                           bproj_row[0:1, cs * 512:(cs + 1) * 512],
                           start=True, stop=True)
            p_b = inc(li, "pe")
            wt(DVE, "pe", p_b)
            DVE.tensor_copy(bpb_bf[:, cs * 512:(cs + 1) * 512],
                                        PS_G[:, :])
            v_bpb = cnt["dve"]
            bank["G"]["dve"] = v_bpb

        MROW = [mrow0, mrow1]

        STAT = [
            dict(mu=mu, ssq=ssq, vvar=vvar, sinv=sinv, musin=musin,
                 tcol=stcol, t2col=t2col, lncol=lncol, za=za_bf),
            dict(mu=mu_b, ssq=ssq_b, vvar=vvar_b, sinv=sinv_b, musin=musin_b,
                 tcol=stcol_b, t2col=t2col_b, lncol=lncol_b, za=za_bf_b),
        ]

        def stats_za(buf, semname, semval, sq_guard_dve, musin_guard_pe,
                     za_guard_pe, cs=0):
            S = STAT[cs]
            wt(ACTE, semname, semval)
            wt(ACTE, "dve", sq_guard_dve)
            ACTE.activation(sq_scr[:], XT[buf][:], AF.Square,
                            accum_out=S["ssq"][:])
            ACTE.activation(sq_scr[:], XT[buf][:], AF.Identity,
                            accum_out=S["tcol"][:])
            a1 = cnt["act"]
            wt(DVE, "act", a1)
            DVE.tensor_single_scalar(S["mu"][:], S["tcol"][:], 1.0 / 2048,
                                     ALU.mult)
            DVE.tensor_single_scalar(S["t2col"][:], S["ssq"][:], 1.0 / 2048,
                                     ALU.mult)
            DVE.tensor_tensor(S["tcol"][:], S["mu"][:], S["mu"][:], ALU.mult)
            DVE.tensor_tensor(S["vvar"][:], S["t2col"][:], S["tcol"][:],
                              ALU.subtract)
            v2 = cnt["dve"]
            wt(ACTE, "dve", v2)
            ACTE.activation(S["lncol"][:], S["vvar"][:], AF.Ln, bias=c_eps[:])
            ACTE.activation(S["sinv"][:], S["lncol"][:], AF.Exp, scale=c_nh[:])
            a2 = cnt["act"]
            wt(DVE, "act", a2)
            wt(DVE, "pe", musin_guard_pe)
            DVE.tensor_tensor(S["musin"][:], S["mu"][:], S["sinv"][:], ALU.mult)
            v3 = cnt["dve"]
            wt(DVE, "pe", za_guard_pe)
            DVE.tensor_scalar(S["za"][:], XT[buf][:], S["sinv"][:], None,
                              ALU.mult)
            a3 = cnt["dve"]
            return v2, v3, a3

        def za_transposes(tl, evE_prev, guard_pe_extra=0, cs=0):
            zb = STAT[cs]["za"]
            pes, evs = [0, 0], [0, 0]
            for hh in range(2):
                wt(PE, "dve", evE_prev[hh])
                if guard_pe_extra:
                    wt(PE, "act", guard_pe_extra)
                for k in range(8):
                    ctk = hh * 8 + k
                    li = PE.transpose(
                        PS_E[:, hh * 1024 + k * 128: hh * 1024 + (k + 1) * 128],
                        zb[:, ctk * 128:(ctk + 1) * 128], ident[:])
                pes[hh] = inc(li, "pe")
                wt(DVE, "pe", pes[hh])
                dst = zaT[:, :].rearrange("p (c t) -> p c t", t=512)[
                    :, hh * 8:(hh + 1) * 8, tl * 128:(tl + 1) * 128]
                src = PS_E[:, hh * 1024:(hh + 1) * 1024].rearrange(
                    "p (c t) -> p c t", t=128)
                DVE.tensor_copy(dst, src)
                evs[hh] = cnt["dve"]
            return pes, evs

        WPC = [wpc0, wpc1]

        def mk_precast(ck):
            def go():
                csq, sub = ck // 8, ck % 8
                b0 = ck % 2
                if ck >= 2:
                    wt(GP, "d_scrw%d" % b0, cnt["d_scrw%d" % b0])
                GP.dma_start(WPC[b0][:],
                             wproj_in.ap()[csq].rearrange("p f c -> p (f c)")[
                                 :, sub * 1024:(sub + 1) * 1024]
                             ).then_inc(sems["d_wpc%d" % b0], 16)
                cnt["d_wpc%d" % b0] += 16
                wt(SP, "d_wpc%d" % b0, cnt["d_wpc%d" % b0])
                SP.dma_start(
                    wproj_bf_dram.ap()[csq].rearrange("p f c -> p (f c)")[
                        :, sub * 1024:(sub + 1) * 1024],
                    WPC[b0][:]).then_inc(sems["d_scrw%d" % b0], 16)
                cnt["d_scrw%d" % b0] += 16
            return go
        precast_q = [mk_precast(ck) for ck in range(32)]

        x_t = x_full.ap().rearrange("(n p) c -> n p c", p=128)
        dvex = [0] * 64
        actza = [0] * 64
        evE = [0, 0]
        ktev = [0, 0]
        mrow_ev = [0, 0]
        pfrow_prev = 0
        XT = [xt0, xt1, xt2]
        xwait = {}
        for pi in range(3):
            SP.dma_start(XT[pi][:], x_t[pi]).then_inc(sems["d_xt%d" % pi], 16)
            cnt["d_xt%d" % pi] += 16
            xwait[pi] = cnt["d_xt%d" % pi]

        prev_v2 = [0, 0]
        prev_pfrow = [0, 0]
        prev_zape = [0, 0]
        for ti in range(64):
            g, tl = ti // 4, ti % 4
            buf = ti % 3
            cs = ti % 2
            semname = "d_xt%d" % buf
            semval = xwait[ti]
            v2, vmus, a3 = stats_za(buf, semname, semval, prev_v2[cs],
                                    prev_pfrow[cs], prev_zape[cs], cs=cs)
            prev_v2[cs] = v2
            dvex[ti] = max(v2, a3)
            actza[ti] = 0
            wt(PE, "dve", a3)
            zpes, evE = za_transposes(tl, evE, cs=cs)
            prev_zape[cs] = zpes[1]
            wt(PE, "dve", vmus)
            wt(PE, "act", a_slat if ti == 0 else 0)
            wt(PE, "dve", mrow_ev[(g - 1) % 2] if g >= 1 else 0)
            li = PE.transpose(PS_F[0:1, tl * 128:(tl + 1) * 128],
                              STAT[cs]["musin"][:], identf[:])
            pfrow = inc(li, "pe")
            prev_pfrow[cs] = pfrow
            if ti + 3 < 64:
                wt(SP, "dve", dvex[ti])
                nb = (ti + 3) % 3
                SP.dma_start(XT[nb][:], x_t[ti + 3]
                             ).then_inc(sems["d_xt%d" % nb], 16)
                cnt["d_xt%d" % nb] += 16
                xwait[ti + 3] = cnt["d_xt%d" % nb]
            if ti % 2 == 1 and precast_q:
                precast_q.pop(0)()
            if tl == 3:
                wt(DVE, "pe", pfrow)
                DVE.tensor_copy(MROW[g % 2][:],
                                                     PS_F[0:1, 0:512])
                mrow_ev[g % 2] = cnt["dve"]
                wt(PE, "dve", evE[1])
                wt(PE, "dve", mrow_ev[g % 2])
                if g >= 2:
                    wt(PE, "act", ktev[g % 2])
                for ct in range(NCT):
                    PE.matmul(PS_AB[g % 2][:, :],
                              wlat_bf[:, ct * 128:(ct + 1) * 128],
                              zaT[:, ct * 512:(ct + 1) * 512],
                              start=(ct == 0), stop=False)
                li = PE.matmul(PS_AB[g % 2][:, :], slat_neg[:], MROW[g % 2][:],
                               start=False, stop=True)
                p_kt = inc(li, "pe")
                wt(ACTE, "pe", p_kt)
                ACTE.activation(
                    kT_sb[:, g * 512:(g + 1) * 512], PS_AB[g % 2][:, :],
                    AF.Identity, bias=blat_col[:])
                ktev[g % 2] = cnt["act"]
        kT_done = max(ktev)
        bank["A"]["act"] = ktev[0]
        bank["B"]["act"] = ktev[1]

        wt(DVE, "act", kT_done)
        DVE.reduce_sum(tcol[:], kT_sb[:], axis=AX.X)
        DVE.tensor_single_scalar(kmean[:], tcol[:], 1.0 / T, ALU.mult)
        v_km = cnt["dve"]
        wt(DVE, "dve", v_km)
        DVE.tensor_scalar(kc_sb[:], kT_sb[:], kmean[:], None,
                                     ALU.subtract)
        v_kc = cnt["dve"]
        knat_ev = [evE[0], evE[1]]
        for bb in range(8):
            hh = bb % 2
            wt(PE, "act", kT_done)
            wt(PE, "dve", knat_ev[hh])
            for k in range(8):
                si = bb * 8 + k
                li = PE.transpose(
                    PS_E[:, hh * 1024 + k * 128: hh * 1024 + (k + 1) * 128],
                    kT_sb[:, si * 128:(si + 1) * 128], ident[:])
            p_t = inc(li, "pe")
            wt(DVE, "pe", p_t)
            DVE.tensor_copy(
                knat[:, bb * 1024:(bb + 1) * 1024],
                PS_E[:, hh * 1024:(hh + 1) * 1024])
            knat_ev[hh] = cnt["dve"]
        knat_done = max(knat_ev)

        if upto == "a0k":
            if debug:
                wt(SP, "dve", v_kc)
                SP.dma_start(dbg_kt.ap(), kc_sb[:]).then_inc(sems["d_out"], 16)
                cnt["d_out"] += 16
                wt(SP, "dve", knat_done)
                SP.dma_start(dbg_knat.ap(), knat[:]).then_inc(sems["d_out"], 16)
                cnt["d_out"] += 16
                SP.dma_start(dbg_qt.ap()[:, 0:T], kT_sb[:]).then_inc(sems["d_out"], 16)
                cnt["d_out"] += 16
                SP.dma_start(dbg_qt.ap()[:, T:T + 8192], zaT[:]).then_inc(sems["d_out"], 16)
                cnt["d_out"] += 16
                SP.dma_start(dbg_qt.ap()[0:1, T + 8192:T + 8192 + 512],
                             mrow1[:]).then_inc(sems["d_out"], 16)
                cnt["d_out"] += 16
            wt(SP, "d_out", cnt["d_out"])
            return nc

        xo_t = x_own.ap().rearrange("(n p) c -> n p c", p=128)
        WDH = [wdh_bf0, wdh_bf1]
        qev = [0, 0]
        wd_used = [0, 0]
        sd_ev = [0] * NH
        evE_q = knat_ev
        mrow_evq = [mrow_ev[0], mrow_ev[1]]
        dvex_q = [0] * 16
        actza_q = [0] * 16
        pfrow_q = 0

        wt(SP, "dve", max(dvex))
        xwait_q = {}
        for pi in range(3):
            SP.dma_start(XT[pi][:], xo_t[pi]).then_inc(sems["d_xt%d" % pi], 16)
            cnt["d_xt%d" % pi] += 16
            xwait_q[pi] = cnt["d_xt%d" % pi]

        for og in range(4):
            for tl in range(4):
                ti = og * 4 + tl
                buf = ti % 3
                cs = ti % 2
                semname = "d_xt%d" % buf
                semval = xwait_q[ti]
                v2, vmus, a3 = stats_za(buf, semname, semval, prev_v2[cs],
                                        prev_pfrow[cs], prev_zape[cs], cs=cs)
                prev_v2[cs] = v2
                dvex_q[ti] = max(v2, a3)
                actza_q[ti] = 0
                wt(PE, "dve", a3)
                if tl == 0 and og >= 1:
                    wt(PE, "act", max(qev))
                zpes, evE_q = za_transposes(tl, evE_q, cs=cs)
                prev_zape[cs] = zpes[1]
                wt(PE, "dve", vmus)
                wt(PE, "dve", mrow_evq[(og - 1) % 2] if og >= 1 else mrow_ev[1])
                if og == 1 and tl == 0:
                    wt(PE, "act", sd_ev[NH - 1])
                li = PE.transpose(PS_F[0:1, tl * 128:(tl + 1) * 128],
                                  STAT[cs]["musin"][:], identf[:])
                pfrow_q = inc(li, "pe")
                prev_pfrow[cs] = pfrow_q
                if ti + 3 < 16:
                    wt(SP, "dve", dvex_q[ti])
                    nb = (ti + 3) % 3
                    SP.dma_start(XT[nb][:], xo_t[ti + 3]
                                 ).then_inc(sems["d_xt%d" % nb], 16)
                    cnt["d_xt%d" % nb] += 16
                    xwait_q[ti + 3] = cnt["d_xt%d" % nb]
            wt(DVE, "pe", pfrow_q)
            DVE.tensor_copy(MROW[og % 2][:],
                                                   PS_F[0:1, 0:512])
            mrow_evq[og % 2] = cnt["dve"]
            for h in range(NH):
                wbuf = h % 2
                wt(GP, "pe", wd_used[wbuf])
                GP.dma_start(WDH[wbuf][:],
                             wd_in.ap()[h].rearrange("p c l -> p (c l)")
                             ).then_inc(sems["d_wd%d" % wbuf], 16)
                cnt["d_wd%d" % wbuf] += 16
                wd_dma = cnt["d_wd%d" % wbuf]
                if og == 0:
                    wt(PE, "d_wd%d" % wbuf, wd_dma)
                    wt(PE, "act", sd_ev[h - 1] if h >= 1 else 0)
                    wt(PE, "dve", mrow_evq[og % 2])
                    for ct in range(NCT):
                        li = PE.matmul(PS_F[0:1, 128:256], onescol[:],
                                       WDH[wbuf][:, ct * 128:(ct + 1) * 128],
                                       start=(ct == 0), stop=(ct == NCT - 1))
                    p_sd = inc(li, "pe")
                    wt(ACTE, "pe", p_sd)
                    ACTE.activation(
                        sd_neg[0:1, h * 128:(h + 1) * 128], PS_F[0:1, 128:256],
                        AF.Identity, scale=c_n1[0:1, :])
                    sd_ev[h] = cnt["act"]
                wt(PE, "d_wd%d" % wbuf, wd_dma)
                wt(PE, "dve", max(evE_q))
                wt(PE, "dve", mrow_evq[og % 2])
                wt(PE, "act", sd_ev[h] if og == 0 else sd_ev[NH - 1])
                wt(PE, "act", qev[h % 2])
                for ct in range(NCT):
                    PE.matmul(PS_AB[h % 2][:, :],
                              WDH[wbuf][:, ct * 128:(ct + 1) * 128],
                              zaT[:, ct * 512:(ct + 1) * 512],
                              start=(ct == 0), stop=False)
                li = PE.matmul(PS_AB[h % 2][:, :],
                               sd_neg[0:1, h * 128:(h + 1) * 128],
                               MROW[og % 2][:], start=False, stop=True)
                p_q = inc(li, "pe")
                wd_used[wbuf] = p_q
                wt(ACTE, "pe", p_q)
                ACTE.activation(
                    qT[:, h * TOK + og * 512: h * TOK + (og + 1) * 512],
                    PS_AB[h % 2][:, :], AF.Identity, bias=bd_sc[:, h:h + 1],
                    scale=c_isq[:])
                qev[h % 2] = cnt["act"]
        q_done = max(qev)
        if upto == "a0q":
            if debug:
                wt(SP, "dve", v_kc)
                SP.dma_start(dbg_kt.ap(), kc_sb[:]).then_inc(sems["d_out"], 16)
                cnt["d_out"] += 16
                wt(SP, "dve", knat_done)
                SP.dma_start(dbg_knat.ap(), knat[:]).then_inc(sems["d_out"], 16)
                cnt["d_out"] += 16
                wt(SP, "act", q_done)
                SP.dma_start(dbg_qt.ap(), qT[:]).then_inc(sems["d_out"], 16)
                cnt["d_out"] += 16
            wt(SP, "d_out", cnt["d_out"])
            return nc
        bank["A"]["act"] = max(bank["A"]["act"], qev[0])
        bank["B"]["act"] = max(bank["B"]["act"], qev[1])
        a0q_pe_end = cnt["pe"]
        a0q_dve_end = cnt["dve"]
        a0q_act_end = cnt["act"]

        wt(SP, "dve", v_kc)
        wt(SP, "dve", knat_done)
        SP.dma_start(masks_sb[:, :].rearrange("p (d f) -> p d f", f=512),
                     masks_in.ap().rearrange("d p f -> p d f")
                     ).then_inc(sems["d_mask"], 16)
        cnt["d_mask"] += 16
        mask_dma = cnt["d_mask"]

        WPB = [wp_bf0, wp_bf1]
        pT = [pT0, pT1, pT2]
        y_ev = [a0q_dve_end, a0q_dve_end]
        den_free = 0
        bankF_act = [0]
        wp_used = [a0q_pe_end, a0q_pe_end]
        scr_r1 = 0
        scr_xn = 0
        xn_ev = [evE_q[0], evE_q[1]]
        r1_x_add = 0
        xn_tr_done = 0
        xn_pe_last = 0
        gate_pe = 0
        gate_dve_prev = [0]
        pending_gate = []
        pending_gdve = []
        pending_xnt = []
        p_yd = 0

        wt(PE, "dve", v_kc)
        wt(PE, "dve", knat_done)
        wt(PE, "act", q_done)
        wt(PE, "d_mask", mask_dma)
        for _e in (PE, ACTE, DVE):
            wt(_e, "d_scrw0", cnt["d_scrw0"])
            wt(_e, "d_scrw1", cnt["d_scrw1"])

        for qg in range(4):
            iters = 16 * qg + 16
            for h in range(NH):
                qslice = qT[:, h * TOK + qg * 512: h * TOK + (qg + 1) * 512]
                pe_s = [0] * iters
                act_p = [0] * iters
                pe_y = [0] * (iters + 2)

                def emit_yden(k, h=h):
                    if k == 0:
                        wbank(PE, CDN[h % 2])
                        wt(PE, "act", den_free)
                        wt(PE, "dve", y_ev[h % 2])
                        wt(PE, "act", bankF_act[0])
                    wt(PE, "act", act_p[k])
                    PE.matmul(PS_F[0:1, 0:512], onescol[:],
                              pT[k % 3][:], start=(k == 0), stop=False)
                    li = PE.matmul(PS_CD[h % 2][:, :], knat[:, k * 128:(k + 1) * 128],
                                   pT[k % 3][:], start=(k == 0), stop=False)
                    return inc(li, "pe")

                for i in range(iters):
                    bn = R3N[i % 3]
                    bk = PS_R3[i % 3]
                    wbank(PE, bn)
                    li = PE.matmul(bk[:, :], kc_sb[:, i * 128:(i + 1) * 128],
                                   qslice, start=True, stop=True)
                    pe_s[i] = inc(li, "pe")
                    if i >= 2:
                        pe_y[i - 2] = emit_yden(i - 2)
                    v_m = 0
                    if i >= 16 * qg:
                        d = i - 16 * qg
                        wt(DVE, "pe", pe_s[i])
                        DVE.tensor_tensor(
                            bk[:, :], bk[:, :],
                            masks_sb[:, d * 512:(d + 1) * 512], ALU.add)
                        v_m = cnt["dve"]
                    wt(ACTE, "pe", pe_s[i])
                    if v_m:
                        wt(ACTE, "dve", v_m)
                    if i >= 3:
                        wt(ACTE, "pe", pe_y[i - 3])
                    ACTE.activation(pT[i % 3][:], bk[:, :], AF.Exp)
                    a_e = cnt["act"]
                    act_p[i] = a_e
                    bank[bn]["act"] = a_e
                    bank[bn]["dve"] = v_m
                pe_y[iters - 2] = emit_yden(iters - 2)
                wt(PE, "act", act_p[iters - 1])
                PE.matmul(PS_F[0:1, 0:512], onescol[:],
                          pT[(iters - 1) % 3][:], start=False, stop=True)
                li = PE.matmul(PS_CD[h % 2][:, :],
                               knat[:, (iters - 1) * 128:iters * 128],
                               pT[(iters - 1) % 3][:], start=False, stop=True)
                p_yd = inc(li, "pe")
                wt(ACTE, "pe", p_yd)
                ACTE.activation(dinv_f[:], PS_F[0:1, 0:512], AF.Ln)
                den_free = cnt["act"]
                ACTE.activation(dinv_bf[:], dinv_f[:], AF.Exp, scale=c_n1[0:1, :])
                v_db = cnt["act"]
                wbank(PE, "G")
                wt(PE, "act", v_db)
                li = PE.matmul(PS_G[:, :], onesrow[:], dinv_bf[:], start=True,
                               stop=True)
                p_bc = inc(li, "pe")
                wt(DVE, "pe", p_bc)
                DVE.tensor_copy(denb[:], PS_G[:, :])
                v_den = cnt["dve"]
                bank["G"]["dve"] = v_den
                wt(DVE, "dve", v_den)
                DVE.tensor_tensor(
                    yT_all[:, h * 512:(h + 1) * 512], PS_CD[h % 2][:, :],
                    denb[:], ALU.mult)
                y_ev[h % 2] = cnt["dve"]
                bank[CDN[h % 2]]["dve"] = y_ev[h % 2]

            yT_ready = max(y_ev)
            for qs in range(4):
                qtile = qg * 4 + qs
                if qtile == 0:
                    wt(SP, "act", a0q_act_end)
                    wt(SP, "dve", a0q_dve_end)
                PROJB = [PS_A, PS_B, PS_G]
                PROJN = ["A", "B", "G"]
                proj_pp = [0] * 4

                def proj_mm(cs):
                    wbuf = cs % 2
                    wt(GP, "pe", wp_used[wbuf])
                    if qtile == 0 and cs < 2:
                        wt(GP, "d_scrw0", 16 * 16)
                        wt(GP, "d_scrw1", 16 * 16)
                    GP.dma_start(WPB[wbuf][:], wproj_bf_dram.ap()[cs].rearrange(
                        "p f c -> p (f c)")).then_inc(sems["d_wp%d" % wbuf], 16)
                    cnt["d_wp%d" % wbuf] += 16
                    wt(PE, "d_wp%d" % wbuf, cnt["d_wp%d" % wbuf])
                    wt(PE, "dve", yT_ready)
                    wbank(PE, PROJN[cs % 3])
                    for ft in range(NH):
                        li = PE.matmul(
                            PROJB[cs % 3][:, :],
                            yT_all[:, ft * 512 + qs * 128: ft * 512 + (qs + 1) * 128],
                            WPB[wbuf][:, ft * 512:(ft + 1) * 512],
                            start=(ft == 0), stop=(ft == NH - 1))
                    p_p = inc(li, "pe")
                    wp_used[wbuf] = p_p
                    proj_pp[cs] = p_p

                def proj_evac(cs):
                    wt(DVE, "pe", proj_pp[cs])
                    wt(DVE, "dve", v_bpb)
                    wt(DVE, "d_scr_r1", scr_r1)
                    DVE.tensor_tensor(
                        r1_tile[:, cs * 512:(cs + 1) * 512], PROJB[cs % 3][:, :],
                        bpb_bf[:, cs * 512:(cs + 1) * 512], ALU.add)
                    bank[PROJN[cs % 3]]["dve"] = cnt["dve"]

                for cs in range(3):
                    proj_mm(cs)
                while pending_xnt:
                    pending_xnt.pop(0)()
                while pending_gate:
                    pending_gate.pop(0)()
                proj_evac(0)
                proj_mm(3)
                wt(SP, "dve", r1_x_add)
                SP.dma_start(xq_sb[:], xo_t[qtile]).then_inc(sems["d_xq"], 16)
                cnt["d_xq"] += 16
                for cs in range(1, 4):
                    proj_evac(cs)
                wt(DVE, "d_xq", cnt["d_xq"])
                DVE.tensor_tensor(r1_tile[:], r1_tile[:], xq_sb[:],
                                                 ALU.add)
                r1_x_add = cnt["dve"]
                DVE.reduce_sum(tcol[:], r1_tile[:], axis=AX.X)
                DVE.tensor_single_scalar(mu[:], tcol[:], 1.0 / 2048,
                                                     ALU.mult)
                v_mu1 = cnt["dve"]
                wt(ACTE, "dve", r1_x_add)
                wt(ACTE, "pe", gate_pe)
                wt(ACTE, "d_scr_xn", scr_xn)
                ACTE.activation(xn1_st[:], r1_tile[:], AF.Square,
                                accum_out=ssq[:])
                a_s1 = cnt["act"]
                wt(DVE, "act", a_s1)
                DVE.tensor_single_scalar(t2col[:], ssq[:], 1.0 / 2048, ALU.mult)
                DVE.tensor_tensor(tcol[:], mu[:], mu[:], ALU.mult)
                DVE.tensor_tensor(vvar[:], t2col[:], tcol[:],
                                             ALU.subtract)
                v_v1 = cnt["dve"]
                wt(ACTE, "dve", v_v1)
                ACTE.activation(lncol[:], vvar[:], AF.Ln, bias=c_eps[:])
                ACTE.activation(sinv[:], lncol[:], AF.Exp, scale=c_nh[:])
                a_r1 = cnt["act"]
                wt(DVE, "act", a_r1)
                wt(DVE, "dve", v_mu1)
                wt(DVE, "pe", xn_pe_last)
                DVE.tensor_scalar(xn1_nat[:], r1_tile[:], mu[:], sinv[:],
                                             ALU.subtract, ALU.mult)
                v_xn = cnt["dve"]

                def _xnt(qtile=qtile, v_xn=v_xn):
                    nonlocal xn_pe_last, xn_tr_done, scr_xn
                    for hh in range(2):
                        wt(PE, "dve", v_xn)
                        wt(PE, "dve", xn_ev[hh])
                        for k in range(8):
                            ctk = hh * 8 + k
                            li = PE.transpose(
                                PS_E[:, hh * 1024 + k * 128: hh * 1024 + (k + 1) * 128],
                                xn1_nat[:, ctk * 128:(ctk + 1) * 128], ident[:])
                        p_x = inc(li, "pe")
                        xn_pe_last = p_x
                        wt(DVE, "pe", p_x)
                        wt(DVE, "d_scr_xn", scr_xn)
                        wt(DVE, "pe", gate_pe)
                        DVE.tensor_copy(
                            xn1_st[:, hh * 1024:(hh + 1) * 1024],
                            PS_E[:, hh * 1024:(hh + 1) * 1024])
                        xn_ev[hh] = cnt["dve"]
                    xn_tr_done = max(xn_ev)
                    wt(SP, "dve", xn_tr_done)
                    SP.dma_start(xn1t_dram.ap()[qtile], xn1_st[:]
                                 ).then_inc(sems["d_scr_xn"], 16)
                    cnt["d_scr_xn"] += 16
                    scr_xn = cnt["d_scr_xn"]
                pending_xnt.append(_xnt)
                def _gate_mm(qtile=qtile):
                    nonlocal gate_pe
                    wt(PE, "dve", xn_tr_done)
                    wt(PE, "act", den_free)
                    wt(PE, "act", bankF_act[0])
                    wt(PE, "d_setupg", setup_gp)
                    for ct in range(NCT):
                        li = PE.matmul(PS_F[:, 0:32],
                                       xn1_st[:, ct * 128:(ct + 1) * 128],
                                       gate_bf[:, ct * 32:(ct + 1) * 32],
                                       start=(ct == 0), stop=(ct == NCT - 1))
                    p_lg = inc(li, "pe")
                    gate_pe = p_lg
                    wt(ACTE, "pe", p_lg)
                    wt(ACTE, "dve", gate_dve_prev[0])
                    ACTE.activation(probs[:], PS_F[:, 0:32], AF.Exp,
                                    accum_out=gden[:])
                    a_g = cnt["act"]
                    bankF_act[0] = a_g

                    def _gate_dve(qtile=qtile, a_g=a_g):
                        wt(DVE, "act", a_g)
                        DVE.reciprocal(gdeni[:], gden[:])
                        DVE.tensor_scalar(probs[:], probs[:], gdeni[:], None, ALU.mult)
                        DVE.tensor_copy(gwork[:], probs[:])
                        for i8 in range(8):
                            DVE.reduce_max(gmc[:], gwork[:], axis=AX.X)
                            DVE.tensor_copy(gm8[:, i8:i8 + 1], gmc[:])
                            if i8 < 7:
                                DVE.tensor_scalar(gtmp[:], gwork[:], gmc[:], NEG,
                                                  ALU.is_ge, ALU.mult)
                                DVE.tensor_tensor(gwork[:], gwork[:], gtmp[:], ALU.add)
                        DVE.tensor_copy(gcs[:, 0:1], gm8[:, 0:1])
                        for i8 in range(1, 8):
                            DVE.tensor_tensor(gcs[:, i8:i8 + 1], gcs[:, i8 - 1:i8],
                                              gm8[:, i8:i8 + 1], ALU.add)
                        DVE.tensor_single_scalar(gthr[:], gcs[:, 7:8], 0.8, ALU.mult)
                        DVE.memset(gkeep[:, 0:1], 1.0)
                        DVE.tensor_scalar(gkeep[:, 1:8], gcs[:, 0:7], gthr[:], None,
                                          ALU.is_lt)
                        DVE.tensor_tensor(gtmp[:, 0:8], gm8[:], gkeep[:], ALU.mult)
                        DVE.reduce_sum(gSk[:], gtmp[:, 0:8], axis=AX.X)
                        DVE.tensor_scalar(gwork[:, 0:8], gkeep[:], 1.0, NEG,
                                          ALU.subtract, ALU.mult)
                        DVE.tensor_tensor(gwork[:, 0:8], gwork[:, 0:8], gtmp[:, 0:8],
                                          ALU.add)
                        DVE.tensor_reduce(gtau[:], gwork[:, 0:8], axis=AX.X, op=ALU.min)
                        DVE.reciprocal(gSki[:], gSk[:])
                        DVE.tensor_scalar(gtmp[:], probs[:], gtau[:], None, ALU.is_ge)
                        DVE.tensor_tensor(gtmp[:], gtmp[:], probs[:], ALU.mult)
                        DVE.tensor_scalar(comb[:, qtile * 32:(qtile + 1) * 32], gtmp[:],
                                          gSki[:], None, ALU.mult)
                        gate_dve_prev[0] = cnt["dve"]
                    pending_gdve.append(_gate_dve)
                pending_gate.append(_gate_mm)
                wt(SP, "dve", r1_x_add)
                SP.dma_start(r1_dram.ap()[qtile * 128:(qtile + 1) * 128, :],
                             r1_tile[:]).then_inc(sems["d_scr_r1"], 16)
                cnt["d_scr_r1"] += 16
                scr_r1 = cnt["d_scr_r1"]
                while pending_gdve:
                    pending_gdve.pop(0)()

        while pending_xnt:
            pending_xnt.pop(0)()
        while pending_gate:
            pending_gate.pop(0)()
        while pending_gdve:
            pending_gdve.pop(0)()
        if debug:
            wt(SP, "dve", v_kc)
            SP.dma_start(dbg_kt.ap(), kc_sb[:]).then_inc(sems["d_out"], 16)
            cnt["d_out"] += 16
            wt(SP, "dve", knat_done)
            SP.dma_start(dbg_knat.ap(), knat[:]).then_inc(sems["d_out"], 16)
            cnt["d_out"] += 16
            wt(SP, "act", q_done)
            SP.dma_start(dbg_qt.ap(), qT[:]).then_inc(sems["d_out"], 16)
            cnt["d_out"] += 16

        if upto == "amain":
            wt(SP, "d_scr_r1", cnt["d_scr_r1"])
            wt(SP, "d_scr_xn", cnt["d_scr_xn"])
            wt(SP, "d_out", cnt["d_out"])
            return nc
        a_end_pe = cnt["pe"]
        a_end_act = cnt["act"]
        a_end_dve = cnt["dve"]
        a_end_r1 = cnt["d_scr_r1"]
        a_end_xn = cnt["d_scr_xn"]
        for bn in bank:
            bank[bn] = {"act": 0, "dve": 0}

        WGU = [wgu_bf0, wgu_bf1, wgu_bf2]
        HT = [hT0, hT1]
        R1RD = [r1rd0, r1rd1]
        wgu_used = {}
        wdn_used = a_end_pe
        gsil_guard = a_end_dve
        out_add = [a_end_dve, a_end_dve]
        out_st_used = [0, 0]
        xn1_pe_used = a_end_pe
        out_dma_n = 0
        gb = {k: {"act": a_end_act, "dve": a_end_dve} for k in ["A", "B", "C", "D", "F"]}

        for hf in range(2):
            if hf == 0:
                wt(SP, "d_scr_r1", a_end_r1)
                wt(SP, "d_scr_xn", a_end_xn)
                wt(SP, "d_out", cnt["d_out"])
                wt(SP, "d_xq", cnt["d_xq"])
                wt(SP, "pe", a_end_pe)
                wt(SP, "act", a_end_act)
                wt(SP, "dve", a_end_dve)
                wt(GP, "d_scr_r1", a_end_r1)
                wt(GP, "d_scr_xn", a_end_xn)
                wt(GP, "d_xq", cnt["d_xq"])
                wt(GP, "d_out", cnt["d_out"])
                wt(GP, "pe", a_end_pe)
                wt(GP, "act", a_end_act)
                wt(GP, "dve", a_end_dve)
            else:
                wt(SP, "pe", xn1_pe_used)
            xsrc = xn1t_dram.ap().rearrange("q p f -> p q f")
            for ct in range(NCT):
                SP.dma_start(
                    xn1T_sb[:, ct * 1024:(ct + 1) * 1024].rearrange(
                        "p (q t) -> p q t", t=128),
                    xsrc[:, 8 * hf:8 * hf + 8, ct * 128:(ct + 1) * 128]
                ).then_inc(sems["d_xn1t"], 16)
                cnt["d_xn1t"] += 16
            xn1_ready = cnt["d_xn1t"]
            if hf == 0:
                wt(DVE, "pe", a_end_pe)
                wt(DVE, "act", a_end_act)
                wt(DVE, "d_out", cnt["d_out"])
                wt(DVE, "d_scr_r1", a_end_r1)
                wt(DVE, "d_scr_xn", a_end_xn)
                wt(DVE, "d_xq", cnt["d_xq"])
            else:
                wt(DVE, "d_out0", cnt["d_out0"])
                wt(DVE, "d_out1", cnt["d_out1"])
                wt(DVE, "d_out", cnt["d_out"])
            DVE.engine_nop()
            v_acc0 = cnt["dve"]

            if debug and hf == 0:
                SP.dma_start(dbg_comb.ap(), comb[:]).then_inc(sems["d_out"], 16)
                cnt["d_out"] += 16

            for e in range(NE):
                ebuf = e % 2
                for jc in range(12):
                    if 6 <= jc <= 9:
                        hcg = jc - 6
                        if hcg == 0:
                            wt(GP, "pe", wdn_used)
                        GP.dma_start(
                            wdn_bf[:, hcg * 3072:(hcg + 1) * 3072].rearrange(
                                "p (i h) -> p i h", h=512),
                            wdn_in.ap()[e, hcg].rearrange("i p h -> p i h")
                        ).then_inc(sems["d_wdn"], 16)
                        cnt["d_wdn"] += 16
                        wdn_ready = cnt["d_wdn"]
                    gjc = (hf * NE + e) * 12 + jc
                    wbuf = gjc % 3
                    wt(GP, "pe", wgu_used.get(gjc - 3, 0))
                    GP.dma_start(WGU[wbuf][:], wgu_in.ap()[e, jc].rearrange(
                        "p c j -> p (c j)")).then_inc(sems["d_wgu%d" % wbuf], 16)
                    cnt["d_wgu%d" % wbuf] += 16
                    wgu_dma = cnt["d_wgu%d" % wbuf]
                    for ts in range(2):
                        pbn = ABN[(jc * 2 + ts) % 2]
                        pb = PS_AB[(jc * 2 + ts) % 2]
                        wt(PE, "d_wgu%d" % wbuf, wgu_dma)
                        wt(PE, "d_xn1t", xn1_ready)
                        wt(PE, "act", gb[pbn]["act"])
                        wt(PE, "dve", gb[pbn]["dve"])
                        for ct in range(NCT):
                            li = PE.matmul(
                                pb[:, :], WGU[wbuf][:, ct * 128:(ct + 1) * 128],
                                xn1T_sb[:, ct * 1024 + ts * 512: ct * 1024 + (ts + 1) * 512],
                                start=(ct == 0), stop=(ct == NCT - 1))
                        p_gu = inc(li, "pe")
                        if jc < 6:
                            wt(ACTE, "pe", p_gu)
                            if jc == 0 and ts == 0:
                                wt(ACTE, "dve", gsil_guard)
                            ACTE.activation(
                                g_sil[:, jc * 1024 + ts * 512: jc * 1024 + (ts + 1) * 512],
                                pb[:, :], AF.Silu)
                            a_si = cnt["act"]
                            gb[pbn]["act"] = a_si
                        else:
                            ic = jc - 6
                            wt(DVE, "pe", p_gu)
                            wt(DVE, "act", sil_done)
                            DVE.tensor_tensor(
                                HT[ebuf][:, ic * 1024 + ts * 512: ic * 1024 + (ts + 1) * 512],
                                g_sil[:, ic * 1024 + ts * 512: ic * 1024 + (ts + 1) * 512],
                                pb[:, :], ALU.mult)
                            gb[pbn]["dve"] = cnt["dve"]
                    wgu_used[gjc] = cnt["pe"]
                    if jc == 5:
                        sil_done = cnt["act"]
                hT_done = cnt["dve"]
                gu_end_pe = cnt["pe"]
                gsil_guard = hT_done
                for tt in range(8):
                    for hcg in range(4):
                        pbn = CDN[(tt * 4 + hcg) % 2]
                        pb = PS_CD[(tt * 4 + hcg) % 2]
                        wt(PE, "dve", hT_done)
                        wt(PE, "d_wdn", wdn_ready)
                        wt(PE, "dve", gb[pbn]["dve"])
                        for it in range(6):
                            li = PE.matmul(
                                pb[:, :],
                                HT[ebuf][:, it * 1024 + tt * 128: it * 1024 + (tt + 1) * 128],
                                wdn_bf[:, hcg * 3072 + it * 512: hcg * 3072 + (it + 1) * 512],
                                start=(it == 0), stop=(it == 5))
                        p_dn = inc(li, "pe")
                        wt(DVE, "pe", p_dn)
                        wt(DVE, "dve", v_acc0)
                        ccol = comb[:, (hf * 8 + tt) * 32 + e:
                                    (hf * 8 + tt) * 32 + e + 1]
                        aslc = acc[:, tt * C + hcg * 512: tt * C + (hcg + 1) * 512]
                        if e == 0:
                            DVE.tensor_scalar(aslc, pb[:, :], ccol, None, ALU.mult)
                        else:
                            DVE.scalar_tensor_tensor(aslc, pb[:, :], ccol, aslc,
                                                     ALU.mult, ALU.add)
                        gb[pbn]["dve"] = cnt["dve"]
                wdn_used = cnt["pe"]
                if e == NE - 1:
                    xn1_pe_used = gu_end_pe
            ffn_done = cnt["dve"]

            for tt in range(8):
                qtile = hf * 8 + tt
                rb = tt % 2
                wt(SP, "d_scr_r1", a_end_r1)
                wt(SP, "d_out%d" % rb, cnt["d_out%d" % rb])
                SP.dma_start(R1RD[rb][:],
                             r1_dram.ap()[qtile * 128:(qtile + 1) * 128, :]
                             ).then_inc(sems["d_r1%d" % rb], 16)
                cnt["d_r1%d" % rb] += 16
                wt(DVE, "d_r1%d" % rb, cnt["d_r1%d" % rb])
                wt(DVE, "dve", ffn_done)
                DVE.tensor_tensor(
                    R1RD[rb][:], acc[:, tt * C:(tt + 1) * C], R1RD[rb][:],
                    ALU.add)
                out_add[rb] = cnt["dve"]
                wt(SP, "dve", out_add[rb])
                SP.dma_start(out_own.ap()[qtile * 128:(qtile + 1) * 128, :],
                             R1RD[rb][:]).then_inc(sems["d_out%d" % rb], 16)
                cnt["d_out%d" % rb] += 16
            out_dma_n = cnt["d_out0"] + cnt["d_out1"]

        wt(SP, "d_out", cnt["d_out"])
        wt(SP, "d_out0", cnt["d_out0"])
        wt(SP, "d_out1", cnt["d_out1"])

    return nc


def _prep_shared(inputs):
    f32 = np.float32
    W_lat = np.asarray(inputs["L0_attn_W_lat"], f32)
    W_d = np.asarray(inputs["L0_attn_W_d"], f32)
    W_proj = np.asarray(inputs["L0_attn_W_proj"], f32)
    gate_w = np.asarray(inputs["L0_ffn_gate_weight"], f32)
    W_gu = np.asarray(inputs["L0_ffn_gate_up_proj"], f32)
    W_dn = np.asarray(inputs["L0_ffn_down_proj"], f32)
    d = {}
    d["wlat_in"] = np.ascontiguousarray(
        W_lat.reshape(16, 128, 128).transpose(1, 0, 2))
    d["wd_in"] = np.ascontiguousarray(
        W_d.reshape(16, 128, 16, 128).transpose(2, 1, 0, 3))
    d["wproj_in"] = np.ascontiguousarray(
        W_proj.reshape(16, 128, 4, 512).transpose(2, 1, 0, 3))
    d["gate_in"] = np.ascontiguousarray(
        gate_w.T.reshape(16, 128, 32).transpose(1, 0, 2))
    d["wgu_in"] = np.ascontiguousarray(
        W_gu.reshape(32, 12, 128, 16, 128).transpose(0, 1, 4, 3, 2))
    d["wdn_in"] = np.ascontiguousarray(
        W_dn.reshape(32, 4, 512, 6, 128).transpose(0, 1, 3, 4, 2))
    d["blat_in"] = np.asarray(inputs["L0_attn_b_lat"], f32)
    d["bd_in"] = np.asarray(inputs["L0_attn_b_d"], f32)
    d["bproj_in"] = np.asarray(inputs["L0_attn_b_proj"], f32)
    d["ident_in"] = np.eye(128, dtype=BFnp)
    d["identf_in"] = np.eye(128, dtype=np.float32)
    return d


def _masks_for(j):
    p = np.arange(128)[:, None]
    f = np.arange(512)[None, :]
    out = np.empty((16, 128, 512), dtype=BFnp)
    for dd in range(16):
        out[dd] = np.where(4 * f + j >= 128 * dd + p, np.float32(0.0),
                           np.float32(-1e30)).astype(BFnp)
    return out


def kernel(**inputs):
    x = np.asarray(inputs["L0_attn_x"], np.float32)
    shared = _prep_shared(inputs)
    nc = build(debug=False)
    in_maps = []
    for c in range(8):
        b, j = c // 4, c % 4
        m = dict(shared)
        m["x_full"] = np.ascontiguousarray(x[b])
        m["x_own"] = np.ascontiguousarray(x[b, j::4])
        m["masks_in"] = _masks_for(j)
        in_maps.append(m)
    res = run_bass_kernel_spmd(nc, in_maps, core_ids=list(range(8)))
    out = np.empty((B, T, C), dtype=np.float32)
    for c in range(8):
        b, j = c // 4, c % 4
        out[b, j::4] = np.asarray(res.results[c]["out_own"], np.float32)
    return out


# version 13
# speedup vs baseline: 1.0817x; 1.0817x over previous
import numpy as np
import ml_dtypes
import concourse.bass as bass
import concourse.mybir as mybir
from concourse.bass_utils import run_bass_kernel_spmd

BFnp = ml_dtypes.bfloat16
F32 = mybir.dt.float32
BF16 = mybir.dt.bfloat16
AF = mybir.ActivationFunctionType
ALU = mybir.AluOpType
AX = mybir.AxisListType

B, T, C = 2, 8192, 2048
NH, L = 16, 128
NE = 32
TOK = 2048
NCT = C // 128
NEG = -1.0e30


def build(debug=False, upto="all"):
    nc = bass.Bass("TRN2", target_bir_lowering=False)

    x_full = nc.dram_tensor("x_full", [T, C], F32, kind="ExternalInput")
    x_own = nc.dram_tensor("x_own", [TOK, C], F32, kind="ExternalInput")
    masks_in = nc.dram_tensor("masks_in", [16, 128, 512], BF16, kind="ExternalInput")
    ident_in = nc.dram_tensor("ident_in", [128, 128], BF16, kind="ExternalInput")
    identf_in = nc.dram_tensor("identf_in", [128, 128], F32, kind="ExternalInput")
    wlat_in = nc.dram_tensor("wlat_in", [128, NCT, 128], F32, kind="ExternalInput")
    blat_in = nc.dram_tensor("blat_in", [L], F32, kind="ExternalInput")
    wd_in = nc.dram_tensor("wd_in", [NH, 128, NCT, 128], F32, kind="ExternalInput")
    bd_in = nc.dram_tensor("bd_in", [C], F32, kind="ExternalInput")
    wproj_in = nc.dram_tensor("wproj_in", [4, 128, NCT, 512], F32, kind="ExternalInput")
    bproj_in = nc.dram_tensor("bproj_in", [C], F32, kind="ExternalInput")
    gate_in = nc.dram_tensor("gate_in", [128, NCT, 32], F32, kind="ExternalInput")
    wgu_in = nc.dram_tensor("wgu_in", [NE, 12, 128, NCT, 128], F32, kind="ExternalInput")
    wdn_in = nc.dram_tensor("wdn_in", [NE, 4, 6, 128, 512], F32, kind="ExternalInput")

    out_own = nc.dram_tensor("out_own", [TOK, C], F32, kind="ExternalOutput")
    skind = "ExternalOutput" if debug else "Internal"
    r1_dram = nc.dram_tensor("r1_dram", [TOK, C], F32, kind=skind)
    wproj_bf_dram = nc.dram_tensor("wproj_bf_dram", [4, 128, NCT, 512], BF16,
                                   kind="Internal")
    xn1t_dram = nc.dram_tensor("xn1t_dram", [16, 128, 2048], BF16, kind=skind)
    if debug:
        dbg_kt = nc.dram_tensor("dbg_kt", [128, T], BF16, kind="ExternalOutput")
        dbg_knat = nc.dram_tensor("dbg_knat", [128, T], BF16, kind="ExternalOutput")
        dbg_qt = nc.dram_tensor("dbg_qt", [128, NH * TOK], BF16, kind="ExternalOutput")
        dbg_comb = nc.dram_tensor("dbg_comb", [128, 512], F32, kind="ExternalOutput")

    SB_BASE = 17408
    LIMIT = 229120
    cur = [SB_BASE]

    def sba(name, shape, dt, at=None):
        esz = {F32: 4, BF16: 2}[dt]
        nbytes = int(np.prod(shape[1:])) * esz
        if at is None:
            off = (cur[0] + 63) // 64 * 64
            cur[0] = off + nbytes
            assert cur[0] <= LIMIT, (name, cur[0])
        else:
            off = at
            assert off + nbytes <= LIMIT, (name, off)
        return nc.alloc_sbuf_tensor_at(name, list(shape), dt, offset=off)

    ident = sba("ident", [128, 128], BF16)
    identf = sba("identf", [128, 128], F32)
    onescol = sba("onescol", [128, 1], BF16)
    onesrow = sba("onesrow", [1, 128], BF16)
    c_eps = sba("c_eps", [128, 1], F32)
    c_nh = sba("c_nh", [128, 1], F32)
    c_n1 = sba("c_n1", [128, 1], F32)
    c_isq = sba("c_isq", [128, 1], F32)
    blat_col = sba("blat_col", [128, 1], F32)
    bd_cols = sba("bd_cols", [128, NH], F32)
    bd_sc = sba("bd_sc", [128, NH], F32)
    bproj_row = sba("bproj_row", [1, C], BF16)
    bpb_bf = sba("bpb_bf", [128, C], BF16)
    gate_bf = sba("gate_bf", [128, NCT * 32], BF16)
    comb = sba("comb", [128, 16 * 32], F32)
    mu = sba("mu", [128, 1], F32)
    ssq = sba("ssq", [128, 1], F32)
    vvar = sba("vvar", [128, 1], F32)
    sinv = sba("sinv", [128, 1], F32)
    musin = sba("musin", [128, 1], F32)
    tcol = sba("tcol", [128, 1], F32)
    t2col = sba("t2col", [128, 1], F32)
    lncol = sba("lncol", [128, 1], F32)
    mu_b = sba("mu_b", [128, 1], F32)
    ssq_b = sba("ssq_b", [128, 1], F32)
    vvar_b = sba("vvar_b", [128, 1], F32)
    sinv_b = sba("sinv_b", [128, 1], F32)
    musin_b = sba("musin_b", [128, 1], F32)
    tcol_b = sba("tcol_b", [128, 1], F32)
    t2col_b = sba("t2col_b", [128, 1], F32)
    lncol_b = sba("lncol_b", [128, 1], F32)
    stcol = sba("stcol", [128, 1], F32)
    stcol_b = sba("stcol_b", [128, 1], F32)
    gden = sba("gden", [128, 1], F32)
    gdeni = sba("gdeni", [128, 1], F32)
    gthr = sba("gthr", [128, 1], F32)
    gSk = sba("gSk", [128, 1], F32)
    gSki = sba("gSki", [128, 1], F32)
    gtau = sba("gtau", [128, 1], F32)
    gmc = sba("gmc", [128, 1], F32)
    kmean = sba("kmean", [128, 1], F32)
    probs = sba("probs", [128, 32], F32)
    gwork = sba("gwork", [128, 32], F32)
    gtmp = sba("gtmp", [128, 32], F32)
    gm8 = sba("gm8", [128, 8], F32)
    gcs = sba("gcs", [128, 8], F32)
    gkeep = sba("gkeep", [128, 8], F32)
    PERS_END = cur[0]

    kc_sb = sba("kc_sb", [128, T], BF16)
    knat = sba("knat", [128, T], BF16)
    qT = sba("qT", [128, NH * TOK], BF16)
    wlat_bf = sba("wlat_bf", [128, NCT * 128], BF16)
    WLAT_OFF = cur[0] - 2 * NCT * 128
    slat_neg = sba("slat_neg", [1, 128], BF16)
    dinv_f = sba("dinv_f", [1, 512], F32)
    dinv_bf = sba("dinv_bf", [1, 512], BF16)
    ATTN_FIX_END = cur[0]

    kT_sb = sba("kT_sb", [128, T], BF16)
    xt0 = sba("xt0", [128, C], F32)
    xt1 = sba("xt1", [128, C], F32)
    za_bf = sba("za_bf", [128, C], BF16)
    za_bf_b = sba("za_bf_b", [128, C], BF16)
    zaT = sba("zaT", [128, NCT * 512], BF16)
    sq_scr = sba("sq_scr", [128, C], BF16)
    wdh_bf0 = sba("wdh_bf0", [128, NCT * 128], BF16)
    wdh_bf1 = sba("wdh_bf1", [128, NCT * 128], BF16)
    wpc0 = sba("wpc0", [128, 1024], BF16)
    wpc1 = sba("wpc1", [128, 1024], BF16)
    xt2 = sba("xt2", [128, C], F32)
    mrow0 = sba("mrow0", [1, 512], BF16)
    mrow1 = sba("mrow1", [1, 512], BF16)
    sd_neg = sba("sd_neg", [1, C], BF16)
    A0_END = cur[0]

    cur[0] = ATTN_FIX_END
    masks_sb = sba("masks_sb", [128, 16 * 512], BF16)
    xq_sb = sba("xq_sb", [128, C], F32)
    XQ_OFF = (cur[0] - 4 * C)
    wp_bf0 = sba("wp_bf0", [128, NCT * 512], BF16)
    wp_bf1 = sba("wp_bf1", [128, NCT * 512], BF16)
    yT_all = sba("yT_all", [128, NH * 512], BF16)
    pT0 = sba("pT0", [128, 512], BF16)
    pT1 = sba("pT1", [128, 512], BF16)
    pT2 = sba("pT2", [128, 512], BF16)
    denb = sba("denb", [128, 512], BF16)
    r1_tile = sba("r1_tile", [128, C], F32)
    xn1_nat = sba("xn1_nat", [128, C], BF16)
    xn1_st = sba("xn1_st", [128, C], BF16, at=WLAT_OFF)
    AM_END = cur[0]

    cur[0] = PERS_END
    xn1T_sb = sba("xn1T_sb", [128, NCT * 1024], BF16)
    acc = sba("acc", [128, 8 * C], F32)
    wgu_bf0 = sba("wgu_bf0", [128, NCT * 128], BF16)
    wgu_bf1 = sba("wgu_bf1", [128, NCT * 128], BF16)
    wgu_bf2 = sba("wgu_bf2", [128, NCT * 128], BF16)
    wdn_bf = sba("wdn_bf", [128, 4 * 6 * 512], BF16)
    g_sil = sba("g_sil", [128, 6 * 1024], BF16)
    hT0 = sba("hT0", [128, 6 * 1024], BF16)
    hT1 = sba("hT1", [128, 6 * 1024], BF16)
    r1rd0 = sba("r1rd0", [128, C], F32)
    r1rd1 = sba("r1rd1", [128, C], F32)
    FFN_END = cur[0]

    PS_A = nc.alloc_psum_tensor("PS_A", [128, 512], F32)
    PS_B = nc.alloc_psum_tensor("PS_B", [128, 512], F32)
    PS_G = nc.alloc_psum_tensor("PS_G", [128, 512], F32)
    PS_C = nc.alloc_psum_tensor("PS_C", [128, 512], F32)
    PS_D = nc.alloc_psum_tensor("PS_D", [128, 512], F32)
    PS_F = nc.alloc_psum_tensor("PS_F", [128, 512], F32)
    PS_E = nc.alloc_psum_tensor("PS_E", [128, 2048], BF16)
    PS_AB = [PS_A, PS_B]
    PS_CD = [PS_C, PS_D]
    PS_R3 = [PS_A, PS_B, PS_G]

    bank = {k: {"act": 0, "dve": 0} for k in ["A", "B", "G", "C", "D", "F", "E0", "E1"]}
    R3N = ["A", "B", "G"]
    ABN = ["A", "B"]
    CDN = ["C", "D"]

    sems = {}
    cnt = {}
    for name in ["pe", "act", "dve", "d_xt0", "d_xt1", "d_xt2", "d_wd0", "d_wd1",
                 "d_wp0", "d_wp1", "d_wgu0", "d_wgu1", "d_wgu2", "d_xn1t",
                 "d_wdn", "d_scr_r1", "d_scr_xn", "d_wpc0", "d_wpc1",
                 "d_scrw0", "d_scrw1",
                 "d_r10", "d_r11", "d_xq", "d_scrw", "d_setup", "d_setupg",
                 "d_out", "d_out0", "d_out1", "d_mask"]:
        sems[name] = nc.alloc_semaphore(name="s_" + name)
        cnt[name] = 0

    def inc(instr, name, k=1):
        instr.then_inc(sems[name], k)
        cnt[name] += k
        return cnt[name]

    def wt(eng, name, val):
        if val > 0:
            eng.wait_ge(sems[name], val)

    def wbank(eng, bn):
        wt(eng, "act", bank[bn]["act"])
        wt(eng, "dve", bank[bn]["dve"])

    class ChainEng:

        def __init__(self, eng, name):
            self._eng = eng
            self._name = name

        def wait_ge(self, sem, val):
            return self._eng.wait_ge(sem, val)

        def __getattr__(self, attr):
            fn = getattr(self._eng, attr)

            def call(*a, **k):
                self._eng.wait_ge(sems[self._name], cnt[self._name])
                ins = fn(*a, **k)
                ins.then_inc(sems[self._name], 1)
                cnt[self._name] += 1
                return ins

            return call

    PE, SP, GP = nc.tensor, nc.sync, nc.gpsimd
    ACTE = ChainEng(nc.scalar, "act")
    DVE = ChainEng(nc.vector, "dve")

    with nc.Block() as blk:
        for dst, src in [(ident, ident_in.ap()), (identf, identf_in.ap())]:
            SP.dma_start(dst[:], src).then_inc(sems["d_setup"], 16)
            cnt["d_setup"] += 16
        SP.dma_start(blat_col[:], blat_in.ap().rearrange("(p o) -> p o", o=1),
                     allow_slow_non_contiguous=True).then_inc(sems["d_setup"], 16)
        cnt["d_setup"] += 16
        SP.dma_start(bd_cols[:], bd_in.ap().rearrange("(h p) -> p h", p=128),
                     allow_slow_non_contiguous=True).then_inc(sems["d_setup"], 16)
        cnt["d_setup"] += 16
        GP.dma_start(wlat_bf[:], wlat_in.ap().rearrange("p c l -> p (c l)")
                     ).then_inc(sems["d_setupg"], 16)
        cnt["d_setupg"] += 16
        GP.dma_start(gate_bf[:], gate_in.ap().rearrange("p c e -> p (c e)")
                     ).then_inc(sems["d_setupg"], 16)
        cnt["d_setupg"] += 16
        GP.dma_start(bproj_row[:], bproj_in.ap().rearrange("(o c) -> o c", o=1)
                     ).then_inc(sems["d_setupg"], 16)
        cnt["d_setupg"] += 16
        setup_dma = cnt["d_setup"]
        setup_gp = cnt["d_setupg"]

        DVE.memset(onescol[:], 1.0)
        DVE.memset(onesrow[:], 1.0)
        DVE.memset(c_eps[:], 1e-5)
        DVE.memset(c_nh[:], -0.5)
        DVE.memset(c_n1[:], -1.0)
        DVE.memset(c_isq[:], 1.0 / float(np.sqrt(128.0)))
        wt(DVE, "d_setup", setup_dma)
        wt(DVE, "d_setupg", setup_gp)
        DVE.tensor_single_scalar(
            bd_sc[:], bd_cols[:], 1.0 / float(np.sqrt(128.0)), ALU.mult)
        v_setup = cnt["dve"]

        wt(PE, "d_setup", setup_dma)
        wt(PE, "d_setupg", setup_gp)
        wt(PE, "dve", v_setup)
        for ct in range(NCT):
            li = PE.matmul(PS_F[0:1, 0:128], onescol[:],
                           wlat_bf[:, ct * 128:(ct + 1) * 128],
                           start=(ct == 0), stop=(ct == NCT - 1))
        p_slat = inc(li, "pe")
        wt(ACTE, "pe", p_slat)
        ACTE.activation(slat_neg[:], PS_F[0:1, 0:128], AF.Identity,
                                     scale=c_n1[0:1, :])
        a_slat = cnt["act"]
        bank["F"]["act"] = a_slat
        v_bpb = 0
        for cs in range(4):
            wbank(PE, "G")
            li = PE.matmul(PS_G[:, :], onesrow[:],
                           bproj_row[0:1, cs * 512:(cs + 1) * 512],
                           start=True, stop=True)
            p_b = inc(li, "pe")
            wt(DVE, "pe", p_b)
            DVE.tensor_copy(bpb_bf[:, cs * 512:(cs + 1) * 512],
                                        PS_G[:, :])
            v_bpb = cnt["dve"]
            bank["G"]["dve"] = v_bpb

        MROW = [mrow0, mrow1]

        STAT = [
            dict(mu=mu, ssq=ssq, vvar=vvar, sinv=sinv, musin=musin,
                 tcol=stcol, t2col=t2col, lncol=lncol, za=za_bf),
            dict(mu=mu_b, ssq=ssq_b, vvar=vvar_b, sinv=sinv_b, musin=musin_b,
                 tcol=stcol_b, t2col=t2col_b, lncol=lncol_b, za=za_bf_b),
        ]

        def stats_za(buf, semname, semval, sq_guard_dve, musin_guard_pe,
                     za_guard_pe, cs=0):
            S = STAT[cs]
            wt(DVE, semname, semval)
            wt(DVE, "dve", sq_guard_dve)
            DVE.reduce_sum(S["t2col"][:], XT[buf][:, 1024:2048], axis=AX.X)
            vr = cnt["dve"]
            wt(ACTE, semname, semval)
            wt(ACTE, "dve", sq_guard_dve)
            ACTE.activation(sq_scr[:], XT[buf][:], AF.Square,
                            accum_out=S["ssq"][:])
            ACTE.activation(sq_scr[:, 0:1024], XT[buf][:, 0:1024], AF.Identity,
                            accum_out=S["tcol"][:])
            a1 = cnt["act"]
            wt(DVE, "act", a1)
            DVE.tensor_tensor(S["tcol"][:], S["tcol"][:], S["t2col"][:],
                              ALU.add)
            DVE.tensor_single_scalar(S["mu"][:], S["tcol"][:], 1.0 / 2048,
                                     ALU.mult)
            DVE.tensor_single_scalar(S["t2col"][:], S["ssq"][:], 1.0 / 2048,
                                     ALU.mult)
            DVE.tensor_tensor(S["tcol"][:], S["mu"][:], S["mu"][:], ALU.mult)
            DVE.tensor_tensor(S["vvar"][:], S["t2col"][:], S["tcol"][:],
                              ALU.subtract)
            v2 = cnt["dve"]
            wt(ACTE, "dve", v2)
            ACTE.activation(S["lncol"][:], S["vvar"][:], AF.Ln, bias=c_eps[:])
            ACTE.activation(S["sinv"][:], S["lncol"][:], AF.Exp, scale=c_nh[:])
            a2 = cnt["act"]
            wt(DVE, "act", a2)
            wt(DVE, "pe", musin_guard_pe)
            DVE.tensor_tensor(S["musin"][:], S["mu"][:], S["sinv"][:], ALU.mult)
            v3 = cnt["dve"]
            wt(DVE, "pe", za_guard_pe)
            DVE.tensor_scalar(S["za"][:], XT[buf][:], S["sinv"][:], None,
                              ALU.mult)
            a3 = cnt["dve"]
            return v2, v3, a3

        def za_transposes(tl, evE_prev, guard_pe_extra=0, cs=0):
            zb = STAT[cs]["za"]
            pes, evs = [0, 0], [0, 0]
            for hh in range(2):
                wt(PE, "dve", evE_prev[hh])
                if guard_pe_extra:
                    wt(PE, "act", guard_pe_extra)
                for k in range(8):
                    ctk = hh * 8 + k
                    li = PE.transpose(
                        PS_E[:, hh * 1024 + k * 128: hh * 1024 + (k + 1) * 128],
                        zb[:, ctk * 128:(ctk + 1) * 128], ident[:])
                pes[hh] = inc(li, "pe")
                wt(DVE, "pe", pes[hh])
                dst = zaT[:, :].rearrange("p (c t) -> p c t", t=512)[
                    :, hh * 8:(hh + 1) * 8, tl * 128:(tl + 1) * 128]
                src = PS_E[:, hh * 1024:(hh + 1) * 1024].rearrange(
                    "p (c t) -> p c t", t=128)
                DVE.tensor_copy(dst, src)
                evs[hh] = cnt["dve"]
            return pes, evs

        WPC = [wpc0, wpc1]

        def mk_precast(ck):
            def go():
                csq, sub = ck // 8, ck % 8
                b0 = ck % 2
                if ck >= 2:
                    wt(GP, "d_scrw%d" % b0, cnt["d_scrw%d" % b0])
                GP.dma_start(WPC[b0][:],
                             wproj_in.ap()[csq].rearrange("p f c -> p (f c)")[
                                 :, sub * 1024:(sub + 1) * 1024]
                             ).then_inc(sems["d_wpc%d" % b0], 16)
                cnt["d_wpc%d" % b0] += 16
                wt(SP, "d_wpc%d" % b0, cnt["d_wpc%d" % b0])
                SP.dma_start(
                    wproj_bf_dram.ap()[csq].rearrange("p f c -> p (f c)")[
                        :, sub * 1024:(sub + 1) * 1024],
                    WPC[b0][:]).then_inc(sems["d_scrw%d" % b0], 16)
                cnt["d_scrw%d" % b0] += 16
            return go
        precast_q = [mk_precast(ck) for ck in range(32)]

        x_t = x_full.ap().rearrange("(n p) c -> n p c", p=128)
        dvex = [0] * 64
        actza = [0] * 64
        evE = [0, 0]
        ktev = [0, 0]
        mrow_ev = [0, 0]
        pfrow_prev = 0
        XT = [xt0, xt1, xt2]
        xwait = {}
        for pi in range(3):
            SP.dma_start(XT[pi][:], x_t[pi]).then_inc(sems["d_xt%d" % pi], 16)
            cnt["d_xt%d" % pi] += 16
            xwait[pi] = cnt["d_xt%d" % pi]

        prev_v2 = [0, 0]
        prev_pfrow = [0, 0]
        prev_zape = [0, 0]
        for ti in range(64):
            g, tl = ti // 4, ti % 4
            buf = ti % 3
            cs = ti % 2
            semname = "d_xt%d" % buf
            semval = xwait[ti]
            v2, vmus, a3 = stats_za(buf, semname, semval, prev_v2[cs],
                                    prev_pfrow[cs], prev_zape[cs], cs=cs)
            prev_v2[cs] = v2
            dvex[ti] = max(v2, a3)
            actza[ti] = 0
            wt(PE, "dve", a3)
            zpes, evE = za_transposes(tl, evE, cs=cs)
            prev_zape[cs] = zpes[1]
            wt(PE, "dve", vmus)
            wt(PE, "act", a_slat if ti == 0 else 0)
            wt(PE, "dve", mrow_ev[(g - 1) % 2] if g >= 1 else 0)
            li = PE.transpose(PS_F[0:1, tl * 128:(tl + 1) * 128],
                              STAT[cs]["musin"][:], identf[:])
            pfrow = inc(li, "pe")
            prev_pfrow[cs] = pfrow
            if ti + 3 < 64:
                wt(SP, "dve", dvex[ti])
                nb = (ti + 3) % 3
                SP.dma_start(XT[nb][:], x_t[ti + 3]
                             ).then_inc(sems["d_xt%d" % nb], 16)
                cnt["d_xt%d" % nb] += 16
                xwait[ti + 3] = cnt["d_xt%d" % nb]
            if ti % 2 == 1 and precast_q:
                precast_q.pop(0)()
            if tl == 3:
                wt(DVE, "pe", pfrow)
                DVE.tensor_copy(MROW[g % 2][:],
                                                     PS_F[0:1, 0:512])
                mrow_ev[g % 2] = cnt["dve"]
                wt(PE, "dve", evE[1])
                wt(PE, "dve", mrow_ev[g % 2])
                if g >= 2:
                    wt(PE, "act", ktev[g % 2])
                for ct in range(NCT):
                    PE.matmul(PS_AB[g % 2][:, :],
                              wlat_bf[:, ct * 128:(ct + 1) * 128],
                              zaT[:, ct * 512:(ct + 1) * 512],
                              start=(ct == 0), stop=False)
                li = PE.matmul(PS_AB[g % 2][:, :], slat_neg[:], MROW[g % 2][:],
                               start=False, stop=True)
                p_kt = inc(li, "pe")
                wt(ACTE, "pe", p_kt)
                ACTE.activation(
                    kT_sb[:, g * 512:(g + 1) * 512], PS_AB[g % 2][:, :],
                    AF.Identity, bias=blat_col[:])
                ktev[g % 2] = cnt["act"]
        kT_done = max(ktev)
        bank["A"]["act"] = ktev[0]
        bank["B"]["act"] = ktev[1]

        wt(DVE, "act", kT_done)
        DVE.reduce_sum(tcol[:], kT_sb[:], axis=AX.X)
        DVE.tensor_single_scalar(kmean[:], tcol[:], 1.0 / T, ALU.mult)
        v_km = cnt["dve"]
        wt(DVE, "dve", v_km)
        DVE.tensor_scalar(kc_sb[:], kT_sb[:], kmean[:], None,
                                     ALU.subtract)
        v_kc = cnt["dve"]
        knat_ev = [evE[0], evE[1]]
        for bb in range(8):
            hh = bb % 2
            wt(PE, "act", kT_done)
            wt(PE, "dve", knat_ev[hh])
            for k in range(8):
                si = bb * 8 + k
                li = PE.transpose(
                    PS_E[:, hh * 1024 + k * 128: hh * 1024 + (k + 1) * 128],
                    kT_sb[:, si * 128:(si + 1) * 128], ident[:])
            p_t = inc(li, "pe")
            wt(DVE, "pe", p_t)
            DVE.tensor_copy(
                knat[:, bb * 1024:(bb + 1) * 1024],
                PS_E[:, hh * 1024:(hh + 1) * 1024])
            knat_ev[hh] = cnt["dve"]
        knat_done = max(knat_ev)

        if upto == "a0k":
            if debug:
                wt(SP, "dve", v_kc)
                SP.dma_start(dbg_kt.ap(), kc_sb[:]).then_inc(sems["d_out"], 16)
                cnt["d_out"] += 16
                wt(SP, "dve", knat_done)
                SP.dma_start(dbg_knat.ap(), knat[:]).then_inc(sems["d_out"], 16)
                cnt["d_out"] += 16
                SP.dma_start(dbg_qt.ap()[:, 0:T], kT_sb[:]).then_inc(sems["d_out"], 16)
                cnt["d_out"] += 16
                SP.dma_start(dbg_qt.ap()[:, T:T + 8192], zaT[:]).then_inc(sems["d_out"], 16)
                cnt["d_out"] += 16
                SP.dma_start(dbg_qt.ap()[0:1, T + 8192:T + 8192 + 512],
                             mrow1[:]).then_inc(sems["d_out"], 16)
                cnt["d_out"] += 16
            wt(SP, "d_out", cnt["d_out"])
            return nc

        xo_t = x_own.ap().rearrange("(n p) c -> n p c", p=128)
        WDH = [wdh_bf0, wdh_bf1]
        qev = [0, 0]
        wd_used = [0, 0]
        sd_ev = [0] * NH
        evE_q = knat_ev
        mrow_evq = [mrow_ev[0], mrow_ev[1]]
        dvex_q = [0] * 16
        actza_q = [0] * 16
        pfrow_q = 0

        wt(SP, "dve", max(dvex))
        xwait_q = {}
        for pi in range(3):
            SP.dma_start(XT[pi][:], xo_t[pi]).then_inc(sems["d_xt%d" % pi], 16)
            cnt["d_xt%d" % pi] += 16
            xwait_q[pi] = cnt["d_xt%d" % pi]

        for og in range(4):
            for tl in range(4):
                ti = og * 4 + tl
                buf = ti % 3
                cs = ti % 2
                semname = "d_xt%d" % buf
                semval = xwait_q[ti]
                v2, vmus, a3 = stats_za(buf, semname, semval, prev_v2[cs],
                                        prev_pfrow[cs], prev_zape[cs], cs=cs)
                prev_v2[cs] = v2
                dvex_q[ti] = max(v2, a3)
                actza_q[ti] = 0
                wt(PE, "dve", a3)
                if tl == 0 and og >= 1:
                    wt(PE, "act", max(qev))
                zpes, evE_q = za_transposes(tl, evE_q, cs=cs)
                prev_zape[cs] = zpes[1]
                wt(PE, "dve", vmus)
                wt(PE, "dve", mrow_evq[(og - 1) % 2] if og >= 1 else mrow_ev[1])
                if og == 1 and tl == 0:
                    wt(PE, "act", sd_ev[NH - 1])
                li = PE.transpose(PS_F[0:1, tl * 128:(tl + 1) * 128],
                                  STAT[cs]["musin"][:], identf[:])
                pfrow_q = inc(li, "pe")
                prev_pfrow[cs] = pfrow_q
                if ti + 3 < 16:
                    wt(SP, "dve", dvex_q[ti])
                    nb = (ti + 3) % 3
                    SP.dma_start(XT[nb][:], xo_t[ti + 3]
                                 ).then_inc(sems["d_xt%d" % nb], 16)
                    cnt["d_xt%d" % nb] += 16
                    xwait_q[ti + 3] = cnt["d_xt%d" % nb]
            wt(DVE, "pe", pfrow_q)
            DVE.tensor_copy(MROW[og % 2][:],
                                                   PS_F[0:1, 0:512])
            mrow_evq[og % 2] = cnt["dve"]
            for h in range(NH):
                wbuf = h % 2
                wt(GP, "pe", wd_used[wbuf])
                GP.dma_start(WDH[wbuf][:],
                             wd_in.ap()[h].rearrange("p c l -> p (c l)")
                             ).then_inc(sems["d_wd%d" % wbuf], 16)
                cnt["d_wd%d" % wbuf] += 16
                wd_dma = cnt["d_wd%d" % wbuf]
                if og == 0:
                    wt(PE, "d_wd%d" % wbuf, wd_dma)
                    wt(PE, "act", sd_ev[h - 1] if h >= 1 else 0)
                    wt(PE, "dve", mrow_evq[og % 2])
                    for ct in range(NCT):
                        li = PE.matmul(PS_F[0:1, 128:256], onescol[:],
                                       WDH[wbuf][:, ct * 128:(ct + 1) * 128],
                                       start=(ct == 0), stop=(ct == NCT - 1))
                    p_sd = inc(li, "pe")
                    wt(ACTE, "pe", p_sd)
                    ACTE.activation(
                        sd_neg[0:1, h * 128:(h + 1) * 128], PS_F[0:1, 128:256],
                        AF.Identity, scale=c_n1[0:1, :])
                    sd_ev[h] = cnt["act"]
                wt(PE, "d_wd%d" % wbuf, wd_dma)
                wt(PE, "dve", max(evE_q))
                wt(PE, "dve", mrow_evq[og % 2])
                wt(PE, "act", sd_ev[h] if og == 0 else sd_ev[NH - 1])
                wt(PE, "act", qev[h % 2])
                for ct in range(NCT):
                    PE.matmul(PS_AB[h % 2][:, :],
                              WDH[wbuf][:, ct * 128:(ct + 1) * 128],
                              zaT[:, ct * 512:(ct + 1) * 512],
                              start=(ct == 0), stop=False)
                li = PE.matmul(PS_AB[h % 2][:, :],
                               sd_neg[0:1, h * 128:(h + 1) * 128],
                               MROW[og % 2][:], start=False, stop=True)
                p_q = inc(li, "pe")
                wd_used[wbuf] = p_q
                wt(ACTE, "pe", p_q)
                ACTE.activation(
                    qT[:, h * TOK + og * 512: h * TOK + (og + 1) * 512],
                    PS_AB[h % 2][:, :], AF.Identity, bias=bd_sc[:, h:h + 1],
                    scale=c_isq[:])
                qev[h % 2] = cnt["act"]
        q_done = max(qev)
        if upto == "a0q":
            if debug:
                wt(SP, "dve", v_kc)
                SP.dma_start(dbg_kt.ap(), kc_sb[:]).then_inc(sems["d_out"], 16)
                cnt["d_out"] += 16
                wt(SP, "dve", knat_done)
                SP.dma_start(dbg_knat.ap(), knat[:]).then_inc(sems["d_out"], 16)
                cnt["d_out"] += 16
                wt(SP, "act", q_done)
                SP.dma_start(dbg_qt.ap(), qT[:]).then_inc(sems["d_out"], 16)
                cnt["d_out"] += 16
            wt(SP, "d_out", cnt["d_out"])
            return nc
        bank["A"]["act"] = max(bank["A"]["act"], qev[0])
        bank["B"]["act"] = max(bank["B"]["act"], qev[1])
        a0q_pe_end = cnt["pe"]
        a0q_dve_end = cnt["dve"]
        a0q_act_end = cnt["act"]

        wt(SP, "dve", v_kc)
        wt(SP, "dve", knat_done)
        SP.dma_start(masks_sb[:, :].rearrange("p (d f) -> p d f", f=512),
                     masks_in.ap().rearrange("d p f -> p d f")
                     ).then_inc(sems["d_mask"], 16)
        cnt["d_mask"] += 16
        mask_dma = cnt["d_mask"]

        WPB = [wp_bf0, wp_bf1]
        pT = [pT0, pT1, pT2]
        y_ev = [a0q_dve_end, a0q_dve_end]
        den_free = 0
        bankF_act = [0]
        wp_used = [a0q_pe_end, a0q_pe_end]
        scr_r1 = 0
        scr_xn = 0
        xn_ev = [evE_q[0], evE_q[1]]
        r1_x_add = 0
        xn_tr_done = 0
        xn_pe_last = 0
        gate_pe = 0
        gate_dve_prev = [0]
        pending_gate = []
        pending_gdve = []
        pending_xnt = []
        pending_bcast = []
        p_yd = 0

        wt(PE, "dve", v_kc)
        wt(PE, "dve", knat_done)
        wt(PE, "act", q_done)
        wt(PE, "d_mask", mask_dma)
        for _e in (PE, ACTE, DVE):
            wt(_e, "d_scrw0", cnt["d_scrw0"])
            wt(_e, "d_scrw1", cnt["d_scrw1"])

        for qg in range(4):
            iters = 16 * qg + 16
            for h in range(NH):
                qslice = qT[:, h * TOK + qg * 512: h * TOK + (qg + 1) * 512]
                pe_s = [0] * iters
                act_p = [0] * iters
                pe_y = [0] * (iters + 2)

                def emit_yden(k, h=h):
                    if k == 0:
                        wbank(PE, CDN[h % 2])
                        wt(PE, "act", den_free)
                        wt(PE, "dve", y_ev[h % 2])
                        wt(PE, "act", bankF_act[0])
                    wt(PE, "act", act_p[k])
                    PE.matmul(PS_F[0:1, 0:512], onescol[:],
                              pT[k % 3][:], start=(k == 0), stop=False)
                    li = PE.matmul(PS_CD[h % 2][:, :], knat[:, k * 128:(k + 1) * 128],
                                   pT[k % 3][:], start=(k == 0), stop=False)
                    return inc(li, "pe")

                for i in range(iters):
                    if i == 2 and pending_bcast:
                        pending_bcast.pop(0)()
                    bn = R3N[i % 3]
                    bk = PS_R3[i % 3]
                    wbank(PE, bn)
                    li = PE.matmul(bk[:, :], kc_sb[:, i * 128:(i + 1) * 128],
                                   qslice, start=True, stop=True)
                    pe_s[i] = inc(li, "pe")
                    if i >= 2:
                        pe_y[i - 2] = emit_yden(i - 2)
                    v_m = 0
                    if i >= 16 * qg:
                        d = i - 16 * qg
                        wt(DVE, "pe", pe_s[i])
                        DVE.tensor_tensor(
                            bk[:, :], bk[:, :],
                            masks_sb[:, d * 512:(d + 1) * 512], ALU.add)
                        v_m = cnt["dve"]
                    wt(ACTE, "pe", pe_s[i])
                    if v_m:
                        wt(ACTE, "dve", v_m)
                    if i >= 3:
                        wt(ACTE, "pe", pe_y[i - 3])
                    ACTE.activation(pT[i % 3][:], bk[:, :], AF.Exp)
                    a_e = cnt["act"]
                    act_p[i] = a_e
                    bank[bn]["act"] = a_e
                    bank[bn]["dve"] = v_m
                pe_y[iters - 2] = emit_yden(iters - 2)
                wt(PE, "act", act_p[iters - 1])
                PE.matmul(PS_F[0:1, 0:512], onescol[:],
                          pT[(iters - 1) % 3][:], start=False, stop=True)
                li = PE.matmul(PS_CD[h % 2][:, :],
                               knat[:, (iters - 1) * 128:iters * 128],
                               pT[(iters - 1) % 3][:], start=False, stop=True)
                p_yd = inc(li, "pe")
                wt(ACTE, "pe", p_yd)
                ACTE.activation(dinv_f[:], PS_F[0:1, 0:512], AF.Ln)
                den_free = cnt["act"]
                ACTE.activation(dinv_bf[:], dinv_f[:], AF.Exp, scale=c_n1[0:1, :])
                v_db = cnt["act"]

                def bcast_yev(h=h, v_db=v_db):
                    wbank(PE, "G")
                    wt(PE, "act", v_db)
                    li = PE.matmul(PS_G[:, :], onesrow[:], dinv_bf[:],
                                   start=True, stop=True)
                    p_bc = inc(li, "pe")
                    wt(DVE, "pe", p_bc)
                    DVE.tensor_copy(denb[:], PS_G[:, :])
                    v_den = cnt["dve"]
                    bank["G"]["dve"] = v_den
                    wt(DVE, "dve", v_den)
                    DVE.tensor_tensor(
                        yT_all[:, h * 512:(h + 1) * 512], PS_CD[h % 2][:, :],
                        denb[:], ALU.mult)
                    y_ev[h % 2] = cnt["dve"]
                    bank[CDN[h % 2]]["dve"] = y_ev[h % 2]
                if h == NH - 1:
                    bcast_yev()
                else:
                    pending_bcast.append(bcast_yev)

            yT_ready = max(y_ev)
            for qs in range(4):
                qtile = qg * 4 + qs
                if qtile == 0:
                    wt(SP, "act", a0q_act_end)
                    wt(SP, "dve", a0q_dve_end)
                PROJB = [PS_A, PS_B, PS_G]
                PROJN = ["A", "B", "G"]
                proj_pp = [0] * 4

                def proj_mm(cs):
                    wbuf = cs % 2
                    wt(GP, "pe", wp_used[wbuf])
                    if qtile == 0 and cs < 2:
                        wt(GP, "d_scrw0", 16 * 16)
                        wt(GP, "d_scrw1", 16 * 16)
                    GP.dma_start(WPB[wbuf][:], wproj_bf_dram.ap()[cs].rearrange(
                        "p f c -> p (f c)")).then_inc(sems["d_wp%d" % wbuf], 16)
                    cnt["d_wp%d" % wbuf] += 16
                    wt(PE, "d_wp%d" % wbuf, cnt["d_wp%d" % wbuf])
                    wt(PE, "dve", yT_ready)
                    wbank(PE, PROJN[cs % 3])
                    for ft in range(NH):
                        li = PE.matmul(
                            PROJB[cs % 3][:, :],
                            yT_all[:, ft * 512 + qs * 128: ft * 512 + (qs + 1) * 128],
                            WPB[wbuf][:, ft * 512:(ft + 1) * 512],
                            start=(ft == 0), stop=(ft == NH - 1))
                    p_p = inc(li, "pe")
                    wp_used[wbuf] = p_p
                    proj_pp[cs] = p_p

                def proj_evac(cs):
                    wt(DVE, "pe", proj_pp[cs])
                    wt(DVE, "dve", v_bpb)
                    wt(DVE, "d_scr_r1", scr_r1)
                    DVE.tensor_tensor(
                        r1_tile[:, cs * 512:(cs + 1) * 512], PROJB[cs % 3][:, :],
                        bpb_bf[:, cs * 512:(cs + 1) * 512], ALU.add)
                    bank[PROJN[cs % 3]]["dve"] = cnt["dve"]

                for cs in range(3):
                    proj_mm(cs)
                while pending_xnt:
                    pending_xnt.pop(0)()
                while pending_gate:
                    pending_gate.pop(0)()
                proj_evac(0)
                proj_mm(3)
                wt(SP, "dve", r1_x_add)
                SP.dma_start(xq_sb[:], xo_t[qtile]).then_inc(sems["d_xq"], 16)
                cnt["d_xq"] += 16
                for cs in range(1, 4):
                    proj_evac(cs)
                wt(DVE, "d_xq", cnt["d_xq"])
                DVE.tensor_tensor(r1_tile[:], r1_tile[:], xq_sb[:],
                                                 ALU.add)
                r1_x_add = cnt["dve"]
                DVE.reduce_sum(tcol[:], r1_tile[:], axis=AX.X)
                DVE.tensor_single_scalar(mu[:], tcol[:], 1.0 / 2048,
                                                     ALU.mult)
                v_mu1 = cnt["dve"]
                wt(ACTE, "dve", r1_x_add)
                wt(ACTE, "pe", gate_pe)
                wt(ACTE, "d_scr_xn", scr_xn)
                ACTE.activation(xn1_st[:], r1_tile[:], AF.Square,
                                accum_out=ssq[:])
                a_s1 = cnt["act"]
                wt(DVE, "act", a_s1)
                DVE.tensor_single_scalar(t2col[:], ssq[:], 1.0 / 2048, ALU.mult)
                DVE.tensor_tensor(tcol[:], mu[:], mu[:], ALU.mult)
                DVE.tensor_tensor(vvar[:], t2col[:], tcol[:],
                                             ALU.subtract)
                v_v1 = cnt["dve"]
                wt(ACTE, "dve", v_v1)
                ACTE.activation(lncol[:], vvar[:], AF.Ln, bias=c_eps[:])
                ACTE.activation(sinv[:], lncol[:], AF.Exp, scale=c_nh[:])
                a_r1 = cnt["act"]
                wt(DVE, "act", a_r1)
                wt(DVE, "dve", v_mu1)
                wt(DVE, "pe", xn_pe_last)
                DVE.tensor_scalar(xn1_nat[:], r1_tile[:], mu[:], sinv[:],
                                             ALU.subtract, ALU.mult)
                v_xn = cnt["dve"]

                def _xnt(qtile=qtile, v_xn=v_xn):
                    nonlocal xn_pe_last, xn_tr_done, scr_xn
                    for hh in range(2):
                        wt(PE, "dve", v_xn)
                        wt(PE, "dve", xn_ev[hh])
                        for k in range(8):
                            ctk = hh * 8 + k
                            li = PE.transpose(
                                PS_E[:, hh * 1024 + k * 128: hh * 1024 + (k + 1) * 128],
                                xn1_nat[:, ctk * 128:(ctk + 1) * 128], ident[:])
                        p_x = inc(li, "pe")
                        xn_pe_last = p_x
                        wt(DVE, "pe", p_x)
                        wt(DVE, "d_scr_xn", scr_xn)
                        wt(DVE, "pe", gate_pe)
                        DVE.tensor_copy(
                            xn1_st[:, hh * 1024:(hh + 1) * 1024],
                            PS_E[:, hh * 1024:(hh + 1) * 1024])
                        xn_ev[hh] = cnt["dve"]
                    xn_tr_done = max(xn_ev)
                    wt(SP, "dve", xn_tr_done)
                    SP.dma_start(xn1t_dram.ap()[qtile], xn1_st[:]
                                 ).then_inc(sems["d_scr_xn"], 16)
                    cnt["d_scr_xn"] += 16
                    scr_xn = cnt["d_scr_xn"]
                pending_xnt.append(_xnt)
                def _gate_mm(qtile=qtile):
                    nonlocal gate_pe
                    wt(PE, "dve", xn_tr_done)
                    wt(PE, "act", den_free)
                    wt(PE, "act", bankF_act[0])
                    wt(PE, "d_setupg", setup_gp)
                    for ct in range(NCT):
                        li = PE.matmul(PS_F[:, 0:32],
                                       xn1_st[:, ct * 128:(ct + 1) * 128],
                                       gate_bf[:, ct * 32:(ct + 1) * 32],
                                       start=(ct == 0), stop=(ct == NCT - 1))
                    p_lg = inc(li, "pe")
                    gate_pe = p_lg
                    wt(ACTE, "pe", p_lg)
                    wt(ACTE, "dve", gate_dve_prev[0])
                    ACTE.activation(probs[:], PS_F[:, 0:32], AF.Exp,
                                    accum_out=gden[:])
                    a_g = cnt["act"]
                    bankF_act[0] = a_g

                    def _gate_dve(qtile=qtile, a_g=a_g):
                        wt(DVE, "act", a_g)
                        DVE.reciprocal(gdeni[:], gden[:])
                        DVE.tensor_scalar(probs[:], probs[:], gdeni[:], None, ALU.mult)
                        DVE.tensor_copy(gwork[:], probs[:])
                        for i8 in range(8):
                            DVE.reduce_max(gmc[:], gwork[:], axis=AX.X)
                            DVE.tensor_copy(gm8[:, i8:i8 + 1], gmc[:])
                            if i8 < 7:
                                DVE.tensor_scalar(gtmp[:], gwork[:], gmc[:], NEG,
                                                  ALU.is_ge, ALU.mult)
                                DVE.tensor_tensor(gwork[:], gwork[:], gtmp[:], ALU.add)
                        DVE.tensor_copy(gcs[:, 0:1], gm8[:, 0:1])
                        for i8 in range(1, 8):
                            DVE.tensor_tensor(gcs[:, i8:i8 + 1], gcs[:, i8 - 1:i8],
                                              gm8[:, i8:i8 + 1], ALU.add)
                        DVE.tensor_single_scalar(gthr[:], gcs[:, 7:8], 0.8, ALU.mult)
                        DVE.memset(gkeep[:, 0:1], 1.0)
                        DVE.tensor_scalar(gkeep[:, 1:8], gcs[:, 0:7], gthr[:], None,
                                          ALU.is_lt)
                        DVE.tensor_tensor(gtmp[:, 0:8], gm8[:], gkeep[:], ALU.mult)
                        DVE.reduce_sum(gSk[:], gtmp[:, 0:8], axis=AX.X)
                        DVE.tensor_scalar(gwork[:, 0:8], gkeep[:], 1.0, NEG,
                                          ALU.subtract, ALU.mult)
                        DVE.tensor_tensor(gwork[:, 0:8], gwork[:, 0:8], gtmp[:, 0:8],
                                          ALU.add)
                        DVE.tensor_reduce(gtau[:], gwork[:, 0:8], axis=AX.X, op=ALU.min)
                        DVE.reciprocal(gSki[:], gSk[:])
                        DVE.tensor_scalar(gtmp[:], probs[:], gtau[:], None, ALU.is_ge)
                        DVE.tensor_tensor(gtmp[:], gtmp[:], probs[:], ALU.mult)
                        DVE.tensor_scalar(comb[:, qtile * 32:(qtile + 1) * 32], gtmp[:],
                                          gSki[:], None, ALU.mult)
                        gate_dve_prev[0] = cnt["dve"]
                    pending_gdve.append(_gate_dve)
                pending_gate.append(_gate_mm)
                wt(SP, "dve", r1_x_add)
                SP.dma_start(r1_dram.ap()[qtile * 128:(qtile + 1) * 128, :],
                             r1_tile[:]).then_inc(sems["d_scr_r1"], 16)
                cnt["d_scr_r1"] += 16
                scr_r1 = cnt["d_scr_r1"]
                while pending_gdve:
                    pending_gdve.pop(0)()

        while pending_xnt:
            pending_xnt.pop(0)()
        while pending_gate:
            pending_gate.pop(0)()
        while pending_gdve:
            pending_gdve.pop(0)()
        if debug:
            wt(SP, "dve", v_kc)
            SP.dma_start(dbg_kt.ap(), kc_sb[:]).then_inc(sems["d_out"], 16)
            cnt["d_out"] += 16
            wt(SP, "dve", knat_done)
            SP.dma_start(dbg_knat.ap(), knat[:]).then_inc(sems["d_out"], 16)
            cnt["d_out"] += 16
            wt(SP, "act", q_done)
            SP.dma_start(dbg_qt.ap(), qT[:]).then_inc(sems["d_out"], 16)
            cnt["d_out"] += 16

        if upto == "amain":
            wt(SP, "d_scr_r1", cnt["d_scr_r1"])
            wt(SP, "d_scr_xn", cnt["d_scr_xn"])
            wt(SP, "d_out", cnt["d_out"])
            return nc
        a_end_pe = cnt["pe"]
        a_end_act = cnt["act"]
        a_end_dve = cnt["dve"]
        a_end_r1 = cnt["d_scr_r1"]
        a_end_xn = cnt["d_scr_xn"]
        for bn in bank:
            bank[bn] = {"act": 0, "dve": 0}

        WGU = [wgu_bf0, wgu_bf1, wgu_bf2]
        HT = [hT0, hT1]
        R1RD = [r1rd0, r1rd1]
        wgu_used = {}
        wdn_used = a_end_pe
        gsil_guard = a_end_dve
        out_add = [a_end_dve, a_end_dve]
        out_st_used = [0, 0]
        xn1_pe_used = a_end_pe
        out_dma_n = 0
        gb = {k: {"act": a_end_act, "dve": a_end_dve} for k in ["A", "B", "C", "D", "F"]}

        for hf in range(2):
            if hf == 0:
                wt(SP, "d_scr_r1", a_end_r1)
                wt(SP, "d_scr_xn", a_end_xn)
                wt(SP, "d_out", cnt["d_out"])
                wt(SP, "d_xq", cnt["d_xq"])
                wt(SP, "pe", a_end_pe)
                wt(SP, "act", a_end_act)
                wt(SP, "dve", a_end_dve)
                wt(GP, "d_scr_r1", a_end_r1)
                wt(GP, "d_scr_xn", a_end_xn)
                wt(GP, "d_xq", cnt["d_xq"])
                wt(GP, "d_out", cnt["d_out"])
                wt(GP, "pe", a_end_pe)
                wt(GP, "act", a_end_act)
                wt(GP, "dve", a_end_dve)
            else:
                wt(SP, "pe", xn1_pe_used)
            xsrc = xn1t_dram.ap().rearrange("q p f -> p q f")
            for ct in range(NCT):
                SP.dma_start(
                    xn1T_sb[:, ct * 1024:(ct + 1) * 1024].rearrange(
                        "p (q t) -> p q t", t=128),
                    xsrc[:, 8 * hf:8 * hf + 8, ct * 128:(ct + 1) * 128]
                ).then_inc(sems["d_xn1t"], 16)
                cnt["d_xn1t"] += 16
            xn1_ready = cnt["d_xn1t"]
            if hf == 0:
                wt(DVE, "pe", a_end_pe)
                wt(DVE, "act", a_end_act)
                wt(DVE, "d_out", cnt["d_out"])
                wt(DVE, "d_scr_r1", a_end_r1)
                wt(DVE, "d_scr_xn", a_end_xn)
                wt(DVE, "d_xq", cnt["d_xq"])
            else:
                wt(DVE, "d_out0", cnt["d_out0"])
                wt(DVE, "d_out1", cnt["d_out1"])
                wt(DVE, "d_out", cnt["d_out"])
            DVE.engine_nop()
            v_acc0 = cnt["dve"]

            if debug and hf == 0:
                SP.dma_start(dbg_comb.ap(), comb[:]).then_inc(sems["d_out"], 16)
                cnt["d_out"] += 16

            for e in range(NE):
                ebuf = e % 2
                for jc in range(12):
                    if 6 <= jc <= 9:
                        hcg = jc - 6
                        if hcg == 0:
                            wt(GP, "pe", wdn_used)
                        GP.dma_start(
                            wdn_bf[:, hcg * 3072:(hcg + 1) * 3072].rearrange(
                                "p (i h) -> p i h", h=512),
                            wdn_in.ap()[e, hcg].rearrange("i p h -> p i h")
                        ).then_inc(sems["d_wdn"], 16)
                        cnt["d_wdn"] += 16
                        wdn_ready = cnt["d_wdn"]
                    gjc = (hf * NE + e) * 12 + jc
                    wbuf = gjc % 3
                    wt(GP, "pe", wgu_used.get(gjc - 3, 0))
                    GP.dma_start(WGU[wbuf][:], wgu_in.ap()[e, jc].rearrange(
                        "p c j -> p (c j)")).then_inc(sems["d_wgu%d" % wbuf], 16)
                    cnt["d_wgu%d" % wbuf] += 16
                    wgu_dma = cnt["d_wgu%d" % wbuf]
                    for ts in range(2):
                        pbn = ABN[(jc * 2 + ts) % 2]
                        pb = PS_AB[(jc * 2 + ts) % 2]
                        wt(PE, "d_wgu%d" % wbuf, wgu_dma)
                        wt(PE, "d_xn1t", xn1_ready)
                        wt(PE, "act", gb[pbn]["act"])
                        wt(PE, "dve", gb[pbn]["dve"])
                        for ct in range(NCT):
                            li = PE.matmul(
                                pb[:, :], WGU[wbuf][:, ct * 128:(ct + 1) * 128],
                                xn1T_sb[:, ct * 1024 + ts * 512: ct * 1024 + (ts + 1) * 512],
                                start=(ct == 0), stop=(ct == NCT - 1))
                        p_gu = inc(li, "pe")
                        if jc < 6:
                            wt(ACTE, "pe", p_gu)
                            if jc == 0 and ts == 0:
                                wt(ACTE, "dve", gsil_guard)
                            ACTE.activation(
                                g_sil[:, jc * 1024 + ts * 512: jc * 1024 + (ts + 1) * 512],
                                pb[:, :], AF.Silu)
                            a_si = cnt["act"]
                            gb[pbn]["act"] = a_si
                        else:
                            ic = jc - 6
                            wt(DVE, "pe", p_gu)
                            wt(DVE, "act", sil_done)
                            DVE.tensor_tensor(
                                HT[ebuf][:, ic * 1024 + ts * 512: ic * 1024 + (ts + 1) * 512],
                                g_sil[:, ic * 1024 + ts * 512: ic * 1024 + (ts + 1) * 512],
                                pb[:, :], ALU.mult)
                            gb[pbn]["dve"] = cnt["dve"]
                    wgu_used[gjc] = cnt["pe"]
                    if jc == 5:
                        sil_done = cnt["act"]
                hT_done = cnt["dve"]
                gu_end_pe = cnt["pe"]
                gsil_guard = hT_done
                for tt in range(8):
                    for hcg in range(4):
                        pbn = CDN[(tt * 4 + hcg) % 2]
                        pb = PS_CD[(tt * 4 + hcg) % 2]
                        wt(PE, "dve", hT_done)
                        wt(PE, "d_wdn", wdn_ready)
                        wt(PE, "dve", gb[pbn]["dve"])
                        for it in range(6):
                            li = PE.matmul(
                                pb[:, :],
                                HT[ebuf][:, it * 1024 + tt * 128: it * 1024 + (tt + 1) * 128],
                                wdn_bf[:, hcg * 3072 + it * 512: hcg * 3072 + (it + 1) * 512],
                                start=(it == 0), stop=(it == 5))
                        p_dn = inc(li, "pe")
                        wt(DVE, "pe", p_dn)
                        wt(DVE, "dve", v_acc0)
                        ccol = comb[:, (hf * 8 + tt) * 32 + e:
                                    (hf * 8 + tt) * 32 + e + 1]
                        aslc = acc[:, tt * C + hcg * 512: tt * C + (hcg + 1) * 512]
                        if e == 0:
                            DVE.tensor_scalar(aslc, pb[:, :], ccol, None, ALU.mult)
                        else:
                            DVE.scalar_tensor_tensor(aslc, pb[:, :], ccol, aslc,
                                                     ALU.mult, ALU.add)
                        gb[pbn]["dve"] = cnt["dve"]
                wdn_used = cnt["pe"]
                if e == NE - 1:
                    xn1_pe_used = gu_end_pe
            ffn_done = cnt["dve"]

            for tt in range(8):
                qtile = hf * 8 + tt
                rb = tt % 2
                wt(SP, "d_scr_r1", a_end_r1)
                wt(SP, "d_out%d" % rb, cnt["d_out%d" % rb])
                SP.dma_start(R1RD[rb][:],
                             r1_dram.ap()[qtile * 128:(qtile + 1) * 128, :]
                             ).then_inc(sems["d_r1%d" % rb], 16)
                cnt["d_r1%d" % rb] += 16
                wt(DVE, "d_r1%d" % rb, cnt["d_r1%d" % rb])
                wt(DVE, "dve", ffn_done)
                DVE.tensor_tensor(
                    R1RD[rb][:], acc[:, tt * C:(tt + 1) * C], R1RD[rb][:],
                    ALU.add)
                out_add[rb] = cnt["dve"]
                wt(SP, "dve", out_add[rb])
                SP.dma_start(out_own.ap()[qtile * 128:(qtile + 1) * 128, :],
                             R1RD[rb][:]).then_inc(sems["d_out%d" % rb], 16)
                cnt["d_out%d" % rb] += 16
            out_dma_n = cnt["d_out0"] + cnt["d_out1"]

        wt(SP, "d_out", cnt["d_out"])
        wt(SP, "d_out0", cnt["d_out0"])
        wt(SP, "d_out1", cnt["d_out1"])

    return nc


def _prep_shared(inputs):
    f32 = np.float32
    W_lat = np.asarray(inputs["L0_attn_W_lat"], f32)
    W_d = np.asarray(inputs["L0_attn_W_d"], f32)
    W_proj = np.asarray(inputs["L0_attn_W_proj"], f32)
    gate_w = np.asarray(inputs["L0_ffn_gate_weight"], f32)
    W_gu = np.asarray(inputs["L0_ffn_gate_up_proj"], f32)
    W_dn = np.asarray(inputs["L0_ffn_down_proj"], f32)
    d = {}
    d["wlat_in"] = np.ascontiguousarray(
        W_lat.reshape(16, 128, 128).transpose(1, 0, 2))
    d["wd_in"] = np.ascontiguousarray(
        W_d.reshape(16, 128, 16, 128).transpose(2, 1, 0, 3))
    d["wproj_in"] = np.ascontiguousarray(
        W_proj.reshape(16, 128, 4, 512).transpose(2, 1, 0, 3))
    d["gate_in"] = np.ascontiguousarray(
        gate_w.T.reshape(16, 128, 32).transpose(1, 0, 2))
    d["wgu_in"] = np.ascontiguousarray(
        W_gu.reshape(32, 12, 128, 16, 128).transpose(0, 1, 4, 3, 2))
    d["wdn_in"] = np.ascontiguousarray(
        W_dn.reshape(32, 4, 512, 6, 128).transpose(0, 1, 3, 4, 2))
    d["blat_in"] = np.asarray(inputs["L0_attn_b_lat"], f32)
    d["bd_in"] = np.asarray(inputs["L0_attn_b_d"], f32)
    d["bproj_in"] = np.asarray(inputs["L0_attn_b_proj"], f32)
    d["ident_in"] = np.eye(128, dtype=BFnp)
    d["identf_in"] = np.eye(128, dtype=np.float32)
    return d


def _masks_for(j):
    p = np.arange(128)[:, None]
    f = np.arange(512)[None, :]
    out = np.empty((16, 128, 512), dtype=BFnp)
    for dd in range(16):
        out[dd] = np.where(4 * f + j >= 128 * dd + p, np.float32(0.0),
                           np.float32(-1e30)).astype(BFnp)
    return out


def kernel(**inputs):
    x = np.asarray(inputs["L0_attn_x"], np.float32)
    shared = _prep_shared(inputs)
    nc = build(debug=False)
    in_maps = []
    for c in range(8):
        b, j = c // 4, c % 4
        m = dict(shared)
        m["x_full"] = np.ascontiguousarray(x[b])
        m["x_own"] = np.ascontiguousarray(x[b, j::4])
        m["masks_in"] = _masks_for(j)
        in_maps.append(m)
    res = run_bass_kernel_spmd(nc, in_maps, core_ids=list(range(8)))
    out = np.empty((B, T, C), dtype=np.float32)
    for c in range(8):
        b, j = c // 4, c % 4
        out[b, j::4] = np.asarray(res.results[c]["out_own"], np.float32)
    return out


# version 14 (final)
# speedup vs baseline: 1.1016x; 1.0185x over previous
import numpy as np
import ml_dtypes
import concourse.bass as bass
import concourse.mybir as mybir
from concourse.bass_utils import run_bass_kernel_spmd

BFnp = ml_dtypes.bfloat16
F32 = mybir.dt.float32
BF16 = mybir.dt.bfloat16
AF = mybir.ActivationFunctionType
ALU = mybir.AluOpType
AX = mybir.AxisListType

B, T, C = 2, 8192, 2048
NH, L = 16, 128
NE = 32
TOK = 2048
NCT = C // 128
NEG = -1.0e30


def build(debug=False, upto="all"):
    nc = bass.Bass("TRN2", target_bir_lowering=False)

    x_full = nc.dram_tensor("x_full", [T, C], F32, kind="ExternalInput")
    x_own = nc.dram_tensor("x_own", [TOK, C], F32, kind="ExternalInput")
    masks_in = nc.dram_tensor("masks_in", [16, 128, 512], BF16, kind="ExternalInput")
    ident_in = nc.dram_tensor("ident_in", [128, 128], BF16, kind="ExternalInput")
    identf_in = nc.dram_tensor("identf_in", [128, 128], F32, kind="ExternalInput")
    wlat_in = nc.dram_tensor("wlat_in", [128, NCT, 128], F32, kind="ExternalInput")
    blat_in = nc.dram_tensor("blat_in", [L], F32, kind="ExternalInput")
    wd_in = nc.dram_tensor("wd_in", [NH, 128, NCT, 128], F32, kind="ExternalInput")
    bd_in = nc.dram_tensor("bd_in", [C], F32, kind="ExternalInput")
    wproj_in = nc.dram_tensor("wproj_in", [4, 128, NCT, 512], F32, kind="ExternalInput")
    bproj_in = nc.dram_tensor("bproj_in", [C], F32, kind="ExternalInput")
    gate_in = nc.dram_tensor("gate_in", [128, NCT, 32], F32, kind="ExternalInput")
    wgu_in = nc.dram_tensor("wgu_in", [NE, 12, 128, NCT, 128], F32, kind="ExternalInput")
    wdn_in = nc.dram_tensor("wdn_in", [NE, 4, 6, 128, 512], F32, kind="ExternalInput")

    out_own = nc.dram_tensor("out_own", [TOK, C], F32, kind="ExternalOutput")
    skind = "ExternalOutput" if debug else "Internal"
    r1_dram = nc.dram_tensor("r1_dram", [TOK, C], F32, kind=skind)
    wproj_bf_dram = nc.dram_tensor("wproj_bf_dram", [4, 128, NCT, 512], BF16,
                                   kind="Internal")
    xn1t_dram = nc.dram_tensor("xn1t_dram", [16, 128, 2048], BF16, kind=skind)
    if debug:
        dbg_kt = nc.dram_tensor("dbg_kt", [128, T], BF16, kind="ExternalOutput")
        dbg_knat = nc.dram_tensor("dbg_knat", [128, T], BF16, kind="ExternalOutput")
        dbg_qt = nc.dram_tensor("dbg_qt", [128, NH * TOK], BF16, kind="ExternalOutput")
        dbg_comb = nc.dram_tensor("dbg_comb", [128, 512], F32, kind="ExternalOutput")

    SB_BASE = 17408
    LIMIT = 229120
    cur = [SB_BASE]

    def sba(name, shape, dt, at=None):
        esz = {F32: 4, BF16: 2}[dt]
        nbytes = int(np.prod(shape[1:])) * esz
        if at is None:
            off = (cur[0] + 63) // 64 * 64
            cur[0] = off + nbytes
            assert cur[0] <= LIMIT, (name, cur[0])
        else:
            off = at
            assert off + nbytes <= LIMIT, (name, off)
        return nc.alloc_sbuf_tensor_at(name, list(shape), dt, offset=off)

    ident = sba("ident", [128, 128], BF16)
    identf = sba("identf", [128, 128], F32)
    onescol = sba("onescol", [128, 1], BF16)
    onesrow = sba("onesrow", [1, 128], BF16)
    c_eps = sba("c_eps", [128, 1], F32)
    c_nh = sba("c_nh", [128, 1], F32)
    c_n1 = sba("c_n1", [128, 1], F32)
    c_isq = sba("c_isq", [128, 1], F32)
    blat_col = sba("blat_col", [128, 1], F32)
    bd_cols = sba("bd_cols", [128, NH], F32)
    bd_sc = sba("bd_sc", [128, NH], F32)
    bproj_row = sba("bproj_row", [1, C], BF16)
    bpb_bf = sba("bpb_bf", [128, C], BF16)
    gate_bf = sba("gate_bf", [128, NCT * 32], BF16)
    comb = sba("comb", [128, 16 * 32], F32)
    mu = sba("mu", [128, 1], F32)
    ssq = sba("ssq", [128, 1], F32)
    vvar = sba("vvar", [128, 1], F32)
    sinv = sba("sinv", [128, 1], F32)
    musin = sba("musin", [128, 1], F32)
    tcol = sba("tcol", [128, 1], F32)
    t2col = sba("t2col", [128, 1], F32)
    lncol = sba("lncol", [128, 1], F32)
    mu_b = sba("mu_b", [128, 1], F32)
    ssq_b = sba("ssq_b", [128, 1], F32)
    vvar_b = sba("vvar_b", [128, 1], F32)
    sinv_b = sba("sinv_b", [128, 1], F32)
    musin_b = sba("musin_b", [128, 1], F32)
    tcol_b = sba("tcol_b", [128, 1], F32)
    t2col_b = sba("t2col_b", [128, 1], F32)
    lncol_b = sba("lncol_b", [128, 1], F32)
    stcol = sba("stcol", [128, 1], F32)
    stcol_b = sba("stcol_b", [128, 1], F32)
    gden = sba("gden", [128, 1], F32)
    gdeni = sba("gdeni", [128, 1], F32)
    gthr = sba("gthr", [128, 1], F32)
    gSk = sba("gSk", [128, 1], F32)
    gSki = sba("gSki", [128, 1], F32)
    gtau = sba("gtau", [128, 1], F32)
    gmc = sba("gmc", [128, 1], F32)
    kmean = sba("kmean", [128, 1], F32)
    probs = sba("probs", [128, 32], F32)
    gwork = sba("gwork", [128, 32], F32)
    gtmp = sba("gtmp", [128, 32], F32)
    gm8 = sba("gm8", [128, 8], F32)
    gcs = sba("gcs", [128, 8], F32)
    gkeep = sba("gkeep", [128, 8], F32)
    PERS_END = cur[0]

    kc_sb = sba("kc_sb", [128, T], BF16)
    knat = sba("knat", [128, T], BF16)
    qT = sba("qT", [128, NH * TOK], BF16)
    wlat_bf = sba("wlat_bf", [128, NCT * 128], BF16)
    WLAT_OFF = cur[0] - 2 * NCT * 128
    slat_neg = sba("slat_neg", [1, 128], BF16)
    dinv_f = sba("dinv_f", [1, 512], F32)
    dinv_bf = sba("dinv_bf", [1, 512], BF16)
    ATTN_FIX_END = cur[0]

    kT_sb = sba("kT_sb", [128, T], BF16)
    xt0 = sba("xt0", [128, C], F32)
    xt1 = sba("xt1", [128, C], F32)
    za_bf = sba("za_bf", [128, C], BF16)
    za_bf_b = sba("za_bf_b", [128, C], BF16)
    zaT = sba("zaT", [128, NCT * 512], BF16)
    sq_scr = sba("sq_scr", [128, C], BF16)
    wdh_bf0 = sba("wdh_bf0", [128, NCT * 128], BF16)
    wdh_bf1 = sba("wdh_bf1", [128, NCT * 128], BF16)
    wpc0 = sba("wpc0", [128, 1024], BF16)
    wpc1 = sba("wpc1", [128, 1024], BF16)
    xt2 = sba("xt2", [128, C], F32)
    mrow0 = sba("mrow0", [1, 512], BF16)
    mrow1 = sba("mrow1", [1, 512], BF16)
    sd_neg = sba("sd_neg", [1, C], BF16)
    A0_END = cur[0]

    cur[0] = ATTN_FIX_END
    masks_sb = sba("masks_sb", [128, 16 * 512], BF16)
    xq_sb = sba("xq_sb", [128, C], F32)
    XQ_OFF = (cur[0] - 4 * C)
    wp_bf0 = sba("wp_bf0", [128, NCT * 512], BF16)
    wp_bf1 = sba("wp_bf1", [128, NCT * 512], BF16)
    yT_all = sba("yT_all", [128, NH * 512], BF16)
    pT0 = sba("pT0", [128, 512], BF16)
    pT1 = sba("pT1", [128, 512], BF16)
    pT2 = sba("pT2", [128, 512], BF16)
    denb = sba("denb", [128, 512], BF16)
    r1_tile = sba("r1_tile", [128, C], F32)
    xn1_nat = sba("xn1_nat", [128, C], BF16)
    xn1_st = sba("xn1_st", [128, C], BF16, at=WLAT_OFF)
    AM_END = cur[0]

    cur[0] = PERS_END
    xn1T_sb = sba("xn1T_sb", [128, NCT * 1024], BF16)
    acc = sba("acc", [128, 8 * C], F32)
    wgu_bf0 = sba("wgu_bf0", [128, NCT * 128], BF16)
    wgu_bf1 = sba("wgu_bf1", [128, NCT * 128], BF16)
    wgu_bf2 = sba("wgu_bf2", [128, NCT * 128], BF16)
    wdn_bf = sba("wdn_bf", [128, 4 * 6 * 512], BF16)
    g_sil = sba("g_sil", [128, 6 * 1024], BF16)
    hT0 = sba("hT0", [128, 6 * 1024], BF16)
    hT1 = sba("hT1", [128, 6 * 1024], BF16)
    r1rd0 = sba("r1rd0", [128, C], F32)
    r1rd1 = sba("r1rd1", [128, C], F32)
    FFN_END = cur[0]

    PS_A = nc.alloc_psum_tensor("PS_A", [128, 512], F32)
    PS_B = nc.alloc_psum_tensor("PS_B", [128, 512], F32)
    PS_G = nc.alloc_psum_tensor("PS_G", [128, 512], F32)
    PS_C = nc.alloc_psum_tensor("PS_C", [128, 512], F32)
    PS_D = nc.alloc_psum_tensor("PS_D", [128, 512], F32)
    PS_F = nc.alloc_psum_tensor("PS_F", [128, 512], F32)
    PS_E = nc.alloc_psum_tensor("PS_E", [128, 2048], BF16)
    PS_AB = [PS_A, PS_B]
    PS_CD = [PS_C, PS_D]
    PS_R3 = [PS_A, PS_B, PS_G]

    bank = {k: {"act": 0, "dve": 0} for k in ["A", "B", "G", "C", "D", "F", "E0", "E1"]}
    R3N = ["A", "B", "G"]
    ABN = ["A", "B"]
    CDN = ["C", "D"]

    sems = {}
    cnt = {}
    for name in ["pe", "act", "dve", "d_xt0", "d_xt1", "d_xt2", "d_wd0", "d_wd1",
                 "d_wp0", "d_wp1", "d_wgu0", "d_wgu1", "d_wgu2", "d_xn1t",
                 "d_wdn", "d_scr_r1", "d_scr_xn", "d_wpc0", "d_wpc1",
                 "d_scrw0", "d_scrw1",
                 "d_r10", "d_r11", "d_xq", "d_scrw", "d_setup", "d_setupg",
                 "d_out", "d_out0", "d_out1", "d_mask"]:
        sems[name] = nc.alloc_semaphore(name="s_" + name)
        cnt[name] = 0

    def inc(instr, name, k=1):
        instr.then_inc(sems[name], k)
        cnt[name] += k
        return cnt[name]

    def wt(eng, name, val):
        if val > 0:
            eng.wait_ge(sems[name], val)

    def wbank(eng, bn):
        wt(eng, "act", bank[bn]["act"])
        wt(eng, "dve", bank[bn]["dve"])

    class ChainEng:

        def __init__(self, eng, name):
            self._eng = eng
            self._name = name

        def wait_ge(self, sem, val):
            return self._eng.wait_ge(sem, val)

        def __getattr__(self, attr):
            fn = getattr(self._eng, attr)

            def call(*a, **k):
                self._eng.wait_ge(sems[self._name], cnt[self._name])
                ins = fn(*a, **k)
                ins.then_inc(sems[self._name], 1)
                cnt[self._name] += 1
                return ins

            return call

    PE, SP, GP = nc.tensor, nc.sync, nc.gpsimd
    ACTE = ChainEng(nc.scalar, "act")
    DVE = ChainEng(nc.vector, "dve")

    with nc.Block() as blk:
        for dst, src in [(ident, ident_in.ap()), (identf, identf_in.ap())]:
            SP.dma_start(dst[:], src).then_inc(sems["d_setup"], 16)
            cnt["d_setup"] += 16
        SP.dma_start(blat_col[:], blat_in.ap().rearrange("(p o) -> p o", o=1),
                     allow_slow_non_contiguous=True).then_inc(sems["d_setup"], 16)
        cnt["d_setup"] += 16
        SP.dma_start(bd_cols[:], bd_in.ap().rearrange("(h p) -> p h", p=128),
                     allow_slow_non_contiguous=True).then_inc(sems["d_setup"], 16)
        cnt["d_setup"] += 16
        GP.dma_start(wlat_bf[:], wlat_in.ap().rearrange("p c l -> p (c l)")
                     ).then_inc(sems["d_setupg"], 16)
        cnt["d_setupg"] += 16
        GP.dma_start(gate_bf[:], gate_in.ap().rearrange("p c e -> p (c e)")
                     ).then_inc(sems["d_setupg"], 16)
        cnt["d_setupg"] += 16
        GP.dma_start(bproj_row[:], bproj_in.ap().rearrange("(o c) -> o c", o=1)
                     ).then_inc(sems["d_setupg"], 16)
        cnt["d_setupg"] += 16
        setup_dma = cnt["d_setup"]
        setup_gp = cnt["d_setupg"]

        DVE.memset(onescol[:], 1.0)
        DVE.memset(onesrow[:], 1.0)
        DVE.memset(c_eps[:], 1e-5)
        DVE.memset(c_nh[:], -0.5)
        DVE.memset(c_n1[:], -1.0)
        DVE.memset(c_isq[:], 1.0 / float(np.sqrt(128.0)))
        wt(DVE, "d_setup", setup_dma)
        wt(DVE, "d_setupg", setup_gp)
        DVE.tensor_single_scalar(
            bd_sc[:], bd_cols[:], 1.0 / float(np.sqrt(128.0)), ALU.mult)
        v_setup = cnt["dve"]

        wt(PE, "d_setup", setup_dma)
        wt(PE, "d_setupg", setup_gp)
        wt(PE, "dve", v_setup)
        for ct in range(NCT):
            li = PE.matmul(PS_F[0:1, 0:128], onescol[:],
                           wlat_bf[:, ct * 128:(ct + 1) * 128],
                           start=(ct == 0), stop=(ct == NCT - 1))
        p_slat = inc(li, "pe")
        wt(ACTE, "pe", p_slat)
        ACTE.activation(slat_neg[:], PS_F[0:1, 0:128], AF.Identity,
                                     scale=c_n1[0:1, :])
        a_slat = cnt["act"]
        bank["F"]["act"] = a_slat
        v_bpb = 0
        for cs in range(4):
            wbank(PE, "G")
            li = PE.matmul(PS_G[:, :], onesrow[:],
                           bproj_row[0:1, cs * 512:(cs + 1) * 512],
                           start=True, stop=True)
            p_b = inc(li, "pe")
            wt(DVE, "pe", p_b)
            DVE.tensor_copy(bpb_bf[:, cs * 512:(cs + 1) * 512],
                                        PS_G[:, :])
            v_bpb = cnt["dve"]
            bank["G"]["dve"] = v_bpb

        MROW = [mrow0, mrow1]

        STAT = [
            dict(mu=mu, ssq=ssq, vvar=vvar, sinv=sinv, musin=musin,
                 tcol=stcol, t2col=t2col, lncol=lncol, za=za_bf),
            dict(mu=mu_b, ssq=ssq_b, vvar=vvar_b, sinv=sinv_b, musin=musin_b,
                 tcol=stcol_b, t2col=t2col_b, lncol=lncol_b, za=za_bf_b),
        ]

        def stats_za(buf, semname, semval, sq_guard_dve, musin_guard_pe,
                     za_guard_pe, cs=0, nxt=None):
            S = STAT[cs]
            wt(ACTE, semname, semval)
            wt(ACTE, "dve", sq_guard_dve)
            ACTE.activation(sq_scr[:], XT[buf][:], AF.Square,
                            accum_out=S["ssq"][:])
            ACTE.activation(sq_scr[:, 0:1024], XT[buf][:, 0:1024], AF.Identity,
                            accum_out=S["tcol"][:])
            a1 = cnt["act"]
            wt(DVE, "act", a1)
            DVE.tensor_tensor(S["tcol"][:], S["tcol"][:], S["t2col"][:],
                              ALU.add)
            DVE.tensor_single_scalar(S["mu"][:], S["tcol"][:], 1.0 / 2048,
                                     ALU.mult)
            DVE.tensor_single_scalar(S["t2col"][:], S["ssq"][:], 1.0 / 2048,
                                     ALU.mult)
            DVE.tensor_tensor(S["tcol"][:], S["mu"][:], S["mu"][:], ALU.mult)
            DVE.tensor_tensor(S["vvar"][:], S["t2col"][:], S["tcol"][:],
                              ALU.subtract)
            v2 = cnt["dve"]
            wt(ACTE, "dve", v2)
            ACTE.activation(S["lncol"][:], S["vvar"][:], AF.Ln, bias=c_eps[:])
            ACTE.activation(S["sinv"][:], S["lncol"][:], AF.Exp, scale=c_nh[:])
            a2 = cnt["act"]
            wt(DVE, "act", a2)
            wt(DVE, "pe", musin_guard_pe)
            DVE.tensor_tensor(S["musin"][:], S["mu"][:], S["sinv"][:], ALU.mult)
            v3 = cnt["dve"]
            wt(DVE, "pe", za_guard_pe)
            DVE.tensor_scalar(S["za"][:], XT[buf][:], S["sinv"][:], None,
                              ALU.mult)
            a3 = cnt["dve"]
            if nxt is not None:
                nb, sn, sv, guard, cs2 = nxt
                wt(DVE, sn, sv)
                wt(DVE, "dve", guard)
                DVE.reduce_sum(STAT[cs2]["t2col"][:], XT[nb][:, 1024:2048],
                               axis=AX.X)
            return v2, v3, a3

        def za_transposes(tl, evE_prev, guard_pe_extra=0, cs=0):
            zb = STAT[cs]["za"]
            pes, evs = [0, 0], [0, 0]
            for hh in range(2):
                wt(PE, "dve", evE_prev[hh])
                if guard_pe_extra:
                    wt(PE, "act", guard_pe_extra)
                for k in range(8):
                    ctk = hh * 8 + k
                    li = PE.transpose(
                        PS_E[:, hh * 1024 + k * 128: hh * 1024 + (k + 1) * 128],
                        zb[:, ctk * 128:(ctk + 1) * 128], ident[:])
                pes[hh] = inc(li, "pe")
                wt(DVE, "pe", pes[hh])
                dst = zaT[:, :].rearrange("p (c t) -> p c t", t=512)[
                    :, hh * 8:(hh + 1) * 8, tl * 128:(tl + 1) * 128]
                src = PS_E[:, hh * 1024:(hh + 1) * 1024].rearrange(
                    "p (c t) -> p c t", t=128)
                DVE.tensor_copy(dst, src)
                evs[hh] = cnt["dve"]
            return pes, evs

        WPC = [wpc0, wpc1]

        def mk_precast(ck):
            def go():
                csq, sub = ck // 8, ck % 8
                b0 = ck % 2
                if ck >= 2:
                    wt(GP, "d_scrw%d" % b0, cnt["d_scrw%d" % b0])
                GP.dma_start(WPC[b0][:],
                             wproj_in.ap()[csq].rearrange("p f c -> p (f c)")[
                                 :, sub * 1024:(sub + 1) * 1024]
                             ).then_inc(sems["d_wpc%d" % b0], 16)
                cnt["d_wpc%d" % b0] += 16
                wt(SP, "d_wpc%d" % b0, cnt["d_wpc%d" % b0])
                SP.dma_start(
                    wproj_bf_dram.ap()[csq].rearrange("p f c -> p (f c)")[
                        :, sub * 1024:(sub + 1) * 1024],
                    WPC[b0][:]).then_inc(sems["d_scrw%d" % b0], 16)
                cnt["d_scrw%d" % b0] += 16
            return go
        precast_q = [mk_precast(ck) for ck in range(32)]

        x_t = x_full.ap().rearrange("(n p) c -> n p c", p=128)
        dvex = [0] * 64
        actza = [0] * 64
        evE = [0, 0]
        ktev = [0, 0]
        mrow_ev = [0, 0]
        pfrow_prev = 0
        XT = [xt0, xt1, xt2]
        xwait = {}
        for pi in range(3):
            SP.dma_start(XT[pi][:], x_t[pi]).then_inc(sems["d_xt%d" % pi], 16)
            cnt["d_xt%d" % pi] += 16
            xwait[pi] = cnt["d_xt%d" % pi]

        prev_v2 = [0, 0]
        prev_pfrow = [0, 0]
        prev_zape = [0, 0]
        wt(DVE, "d_xt0", xwait[0])
        DVE.reduce_sum(STAT[0]["t2col"][:], XT[0][:, 1024:2048], axis=AX.X)
        for ti in range(64):
            g, tl = ti // 4, ti % 4
            buf = ti % 3
            cs = ti % 2
            semname = "d_xt%d" % buf
            semval = xwait[ti]
            nxt = None
            if ti + 1 < 64:
                nb1 = (ti + 1) % 3
                nxt = (nb1, "d_xt%d" % nb1, xwait[ti + 1],
                       prev_v2[(ti + 1) % 2], (ti + 1) % 2)
            v2, vmus, a3 = stats_za(buf, semname, semval, prev_v2[cs],
                                    prev_pfrow[cs], prev_zape[cs], cs=cs,
                                    nxt=nxt)
            prev_v2[cs] = v2
            dvex[ti] = max(v2, a3)
            actza[ti] = 0
            wt(PE, "dve", a3)
            zpes, evE = za_transposes(tl, evE, cs=cs)
            prev_zape[cs] = zpes[1]
            wt(PE, "dve", vmus)
            wt(PE, "act", a_slat if ti == 0 else 0)
            wt(PE, "dve", mrow_ev[(g - 1) % 2] if g >= 1 else 0)
            li = PE.transpose(PS_F[0:1, tl * 128:(tl + 1) * 128],
                              STAT[cs]["musin"][:], identf[:])
            pfrow = inc(li, "pe")
            prev_pfrow[cs] = pfrow
            if ti + 3 < 64:
                wt(SP, "dve", dvex[ti])
                nb = (ti + 3) % 3
                SP.dma_start(XT[nb][:], x_t[ti + 3]
                             ).then_inc(sems["d_xt%d" % nb], 16)
                cnt["d_xt%d" % nb] += 16
                xwait[ti + 3] = cnt["d_xt%d" % nb]
            if ti % 2 == 1 and precast_q:
                precast_q.pop(0)()
            if tl == 3:
                wt(DVE, "pe", pfrow)
                DVE.tensor_copy(MROW[g % 2][:],
                                                     PS_F[0:1, 0:512])
                mrow_ev[g % 2] = cnt["dve"]
                wt(PE, "dve", evE[1])
                wt(PE, "dve", mrow_ev[g % 2])
                if g >= 2:
                    wt(PE, "act", ktev[g % 2])
                for ct in range(NCT):
                    PE.matmul(PS_AB[g % 2][:, :],
                              wlat_bf[:, ct * 128:(ct + 1) * 128],
                              zaT[:, ct * 512:(ct + 1) * 512],
                              start=(ct == 0), stop=False)
                li = PE.matmul(PS_AB[g % 2][:, :], slat_neg[:], MROW[g % 2][:],
                               start=False, stop=True)
                p_kt = inc(li, "pe")
                wt(ACTE, "pe", p_kt)
                ACTE.activation(
                    kT_sb[:, g * 512:(g + 1) * 512], PS_AB[g % 2][:, :],
                    AF.Identity, bias=blat_col[:])
                ktev[g % 2] = cnt["act"]
        kT_done = max(ktev)
        bank["A"]["act"] = ktev[0]
        bank["B"]["act"] = ktev[1]

        wt(DVE, "act", kT_done)
        DVE.reduce_sum(tcol[:], kT_sb[:], axis=AX.X)
        DVE.tensor_single_scalar(kmean[:], tcol[:], 1.0 / T, ALU.mult)
        v_km = cnt["dve"]
        wt(DVE, "dve", v_km)
        DVE.tensor_scalar(kc_sb[:], kT_sb[:], kmean[:], None,
                                     ALU.subtract)
        v_kc = cnt["dve"]
        knat_ev = [evE[0], evE[1]]
        for bb in range(8):
            hh = bb % 2
            wt(PE, "act", kT_done)
            wt(PE, "dve", knat_ev[hh])
            for k in range(8):
                si = bb * 8 + k
                li = PE.transpose(
                    PS_E[:, hh * 1024 + k * 128: hh * 1024 + (k + 1) * 128],
                    kT_sb[:, si * 128:(si + 1) * 128], ident[:])
            p_t = inc(li, "pe")
            wt(DVE, "pe", p_t)
            DVE.tensor_copy(
                knat[:, bb * 1024:(bb + 1) * 1024],
                PS_E[:, hh * 1024:(hh + 1) * 1024])
            knat_ev[hh] = cnt["dve"]
        knat_done = max(knat_ev)

        if upto == "a0k":
            if debug:
                wt(SP, "dve", v_kc)
                SP.dma_start(dbg_kt.ap(), kc_sb[:]).then_inc(sems["d_out"], 16)
                cnt["d_out"] += 16
                wt(SP, "dve", knat_done)
                SP.dma_start(dbg_knat.ap(), knat[:]).then_inc(sems["d_out"], 16)
                cnt["d_out"] += 16
                SP.dma_start(dbg_qt.ap()[:, 0:T], kT_sb[:]).then_inc(sems["d_out"], 16)
                cnt["d_out"] += 16
                SP.dma_start(dbg_qt.ap()[:, T:T + 8192], zaT[:]).then_inc(sems["d_out"], 16)
                cnt["d_out"] += 16
                SP.dma_start(dbg_qt.ap()[0:1, T + 8192:T + 8192 + 512],
                             mrow1[:]).then_inc(sems["d_out"], 16)
                cnt["d_out"] += 16
            wt(SP, "d_out", cnt["d_out"])
            return nc

        xo_t = x_own.ap().rearrange("(n p) c -> n p c", p=128)
        WDH = [wdh_bf0, wdh_bf1]
        qev = [0, 0]
        wd_used = [0, 0]
        sd_ev = [0] * NH
        evE_q = knat_ev
        mrow_evq = [mrow_ev[0], mrow_ev[1]]
        dvex_q = [0] * 16
        actza_q = [0] * 16
        pfrow_q = 0

        wt(SP, "dve", max(dvex))
        xwait_q = {}
        for pi in range(3):
            SP.dma_start(XT[pi][:], xo_t[pi]).then_inc(sems["d_xt%d" % pi], 16)
            cnt["d_xt%d" % pi] += 16
            xwait_q[pi] = cnt["d_xt%d" % pi]
        wt(DVE, "d_xt0", xwait_q[0])
        wt(DVE, "dve", prev_v2[0])
        DVE.reduce_sum(STAT[0]["t2col"][:], XT[0][:, 1024:2048], axis=AX.X)

        for og in range(4):
            for tl in range(4):
                ti = og * 4 + tl
                buf = ti % 3
                cs = ti % 2
                semname = "d_xt%d" % buf
                semval = xwait_q[ti]
                nxt = None
                if ti + 1 < 16:
                    nb1 = (ti + 1) % 3
                    nxt = (nb1, "d_xt%d" % nb1, xwait_q[ti + 1],
                           prev_v2[(ti + 1) % 2], (ti + 1) % 2)
                v2, vmus, a3 = stats_za(buf, semname, semval, prev_v2[cs],
                                        prev_pfrow[cs], prev_zape[cs], cs=cs,
                                        nxt=nxt)
                prev_v2[cs] = v2
                dvex_q[ti] = max(v2, a3)
                actza_q[ti] = 0
                wt(PE, "dve", a3)
                if tl == 0 and og >= 1:
                    wt(PE, "act", max(qev))
                zpes, evE_q = za_transposes(tl, evE_q, cs=cs)
                prev_zape[cs] = zpes[1]
                wt(PE, "dve", vmus)
                wt(PE, "dve", mrow_evq[(og - 1) % 2] if og >= 1 else mrow_ev[1])
                if og == 1 and tl == 0:
                    wt(PE, "act", sd_ev[NH - 1])
                li = PE.transpose(PS_F[0:1, tl * 128:(tl + 1) * 128],
                                  STAT[cs]["musin"][:], identf[:])
                pfrow_q = inc(li, "pe")
                prev_pfrow[cs] = pfrow_q
                if ti + 3 < 16:
                    wt(SP, "dve", dvex_q[ti])
                    nb = (ti + 3) % 3
                    SP.dma_start(XT[nb][:], xo_t[ti + 3]
                                 ).then_inc(sems["d_xt%d" % nb], 16)
                    cnt["d_xt%d" % nb] += 16
                    xwait_q[ti + 3] = cnt["d_xt%d" % nb]
            wt(DVE, "pe", pfrow_q)
            DVE.tensor_copy(MROW[og % 2][:],
                                                   PS_F[0:1, 0:512])
            mrow_evq[og % 2] = cnt["dve"]
            for h in range(NH):
                wbuf = h % 2
                wt(GP, "pe", wd_used[wbuf])
                GP.dma_start(WDH[wbuf][:],
                             wd_in.ap()[h].rearrange("p c l -> p (c l)")
                             ).then_inc(sems["d_wd%d" % wbuf], 16)
                cnt["d_wd%d" % wbuf] += 16
                wd_dma = cnt["d_wd%d" % wbuf]
                if og == 0:
                    wt(PE, "d_wd%d" % wbuf, wd_dma)
                    wt(PE, "act", sd_ev[h - 1] if h >= 1 else 0)
                    wt(PE, "dve", mrow_evq[og % 2])
                    for ct in range(NCT):
                        li = PE.matmul(PS_F[0:1, 128:256], onescol[:],
                                       WDH[wbuf][:, ct * 128:(ct + 1) * 128],
                                       start=(ct == 0), stop=(ct == NCT - 1))
                    p_sd = inc(li, "pe")
                    wt(ACTE, "pe", p_sd)
                    ACTE.activation(
                        sd_neg[0:1, h * 128:(h + 1) * 128], PS_F[0:1, 128:256],
                        AF.Identity, scale=c_n1[0:1, :])
                    sd_ev[h] = cnt["act"]
                wt(PE, "d_wd%d" % wbuf, wd_dma)
                wt(PE, "dve", max(evE_q))
                wt(PE, "dve", mrow_evq[og % 2])
                wt(PE, "act", sd_ev[h] if og == 0 else sd_ev[NH - 1])
                wt(PE, "act", qev[h % 2])
                for ct in range(NCT):
                    PE.matmul(PS_AB[h % 2][:, :],
                              WDH[wbuf][:, ct * 128:(ct + 1) * 128],
                              zaT[:, ct * 512:(ct + 1) * 512],
                              start=(ct == 0), stop=False)
                li = PE.matmul(PS_AB[h % 2][:, :],
                               sd_neg[0:1, h * 128:(h + 1) * 128],
                               MROW[og % 2][:], start=False, stop=True)
                p_q = inc(li, "pe")
                wd_used[wbuf] = p_q
                wt(ACTE, "pe", p_q)
                ACTE.activation(
                    qT[:, h * TOK + og * 512: h * TOK + (og + 1) * 512],
                    PS_AB[h % 2][:, :], AF.Identity, bias=bd_sc[:, h:h + 1],
                    scale=c_isq[:])
                qev[h % 2] = cnt["act"]
        q_done = max(qev)
        if upto == "a0q":
            if debug:
                wt(SP, "dve", v_kc)
                SP.dma_start(dbg_kt.ap(), kc_sb[:]).then_inc(sems["d_out"], 16)
                cnt["d_out"] += 16
                wt(SP, "dve", knat_done)
                SP.dma_start(dbg_knat.ap(), knat[:]).then_inc(sems["d_out"], 16)
                cnt["d_out"] += 16
                wt(SP, "act", q_done)
                SP.dma_start(dbg_qt.ap(), qT[:]).then_inc(sems["d_out"], 16)
                cnt["d_out"] += 16
            wt(SP, "d_out", cnt["d_out"])
            return nc
        bank["A"]["act"] = max(bank["A"]["act"], qev[0])
        bank["B"]["act"] = max(bank["B"]["act"], qev[1])
        a0q_pe_end = cnt["pe"]
        a0q_dve_end = cnt["dve"]
        a0q_act_end = cnt["act"]

        wt(SP, "dve", v_kc)
        wt(SP, "dve", knat_done)
        SP.dma_start(masks_sb[:, :].rearrange("p (d f) -> p d f", f=512),
                     masks_in.ap().rearrange("d p f -> p d f")
                     ).then_inc(sems["d_mask"], 16)
        cnt["d_mask"] += 16
        mask_dma = cnt["d_mask"]

        WPB = [wp_bf0, wp_bf1]
        pT = [pT0, pT1, pT2]
        y_ev = [a0q_dve_end, a0q_dve_end]
        den_free = 0
        bankF_act = [0]
        wp_used = [a0q_pe_end, a0q_pe_end]
        scr_r1 = 0
        scr_xn = 0
        xn_ev = [evE_q[0], evE_q[1]]
        r1_x_add = 0
        xn_tr_done = 0
        xn_pe_last = 0
        gate_pe = 0
        gate_dve_prev = [0]
        pending_gate = []
        pending_gdve = []
        pending_xnt = []
        pending_bcast = []
        p_yd = 0

        wt(PE, "dve", v_kc)
        wt(PE, "dve", knat_done)
        wt(PE, "act", q_done)
        wt(PE, "d_mask", mask_dma)
        for _e in (PE, ACTE, DVE):
            wt(_e, "d_scrw0", cnt["d_scrw0"])
            wt(_e, "d_scrw1", cnt["d_scrw1"])

        for qg in range(4):
            iters = 16 * qg + 16
            for h in range(NH):
                qslice = qT[:, h * TOK + qg * 512: h * TOK + (qg + 1) * 512]
                pe_s = [0] * iters
                act_p = [0] * iters
                pe_y = [0] * (iters + 2)

                def emit_yden(k, h=h):
                    if k == 0:
                        wbank(PE, CDN[h % 2])
                        wt(PE, "act", den_free)
                        wt(PE, "dve", y_ev[h % 2])
                        wt(PE, "act", bankF_act[0])
                    wt(PE, "act", act_p[k])
                    PE.matmul(PS_F[0:1, 0:512], onescol[:],
                              pT[k % 3][:], start=(k == 0), stop=False)
                    li = PE.matmul(PS_CD[h % 2][:, :], knat[:, k * 128:(k + 1) * 128],
                                   pT[k % 3][:], start=(k == 0), stop=False)
                    return inc(li, "pe")

                for i in range(iters):
                    if i == 2 and pending_bcast:
                        pending_bcast.pop(0)()
                    bn = R3N[i % 3]
                    bk = PS_R3[i % 3]
                    wbank(PE, bn)
                    li = PE.matmul(bk[:, :], kc_sb[:, i * 128:(i + 1) * 128],
                                   qslice, start=True, stop=True)
                    pe_s[i] = inc(li, "pe")
                    if i >= 2:
                        pe_y[i - 2] = emit_yden(i - 2)
                    v_m = 0
                    if i >= 16 * qg:
                        d = i - 16 * qg
                        wt(DVE, "pe", pe_s[i])
                        DVE.tensor_tensor(
                            bk[:, :], bk[:, :],
                            masks_sb[:, d * 512:(d + 1) * 512], ALU.add)
                        v_m = cnt["dve"]
                    wt(ACTE, "pe", pe_s[i])
                    if v_m:
                        wt(ACTE, "dve", v_m)
                    if i >= 3:
                        wt(ACTE, "pe", pe_y[i - 3])
                    ACTE.activation(pT[i % 3][:], bk[:, :], AF.Exp)
                    a_e = cnt["act"]
                    act_p[i] = a_e
                    bank[bn]["act"] = a_e
                    bank[bn]["dve"] = v_m
                pe_y[iters - 2] = emit_yden(iters - 2)
                wt(PE, "act", act_p[iters - 1])
                PE.matmul(PS_F[0:1, 0:512], onescol[:],
                          pT[(iters - 1) % 3][:], start=False, stop=True)
                li = PE.matmul(PS_CD[h % 2][:, :],
                               knat[:, (iters - 1) * 128:iters * 128],
                               pT[(iters - 1) % 3][:], start=False, stop=True)
                p_yd = inc(li, "pe")
                wt(ACTE, "pe", p_yd)
                ACTE.activation(dinv_f[:], PS_F[0:1, 0:512], AF.Ln)
                den_free = cnt["act"]
                ACTE.activation(dinv_bf[:], dinv_f[:], AF.Exp, scale=c_n1[0:1, :])
                v_db = cnt["act"]

                def bcast_yev(h=h, v_db=v_db):
                    wbank(PE, "G")
                    wt(PE, "act", v_db)
                    li = PE.matmul(PS_G[:, :], onesrow[:], dinv_bf[:],
                                   start=True, stop=True)
                    p_bc = inc(li, "pe")
                    wt(DVE, "pe", p_bc)
                    DVE.tensor_copy(denb[:], PS_G[:, :])
                    v_den = cnt["dve"]
                    bank["G"]["dve"] = v_den
                    wt(DVE, "dve", v_den)
                    DVE.tensor_tensor(
                        yT_all[:, h * 512:(h + 1) * 512], PS_CD[h % 2][:, :],
                        denb[:], ALU.mult)
                    y_ev[h % 2] = cnt["dve"]
                    bank[CDN[h % 2]]["dve"] = y_ev[h % 2]
                if h == NH - 1:
                    bcast_yev()
                else:
                    pending_bcast.append(bcast_yev)

            yT_ready = max(y_ev)
            for qs in range(4):
                qtile = qg * 4 + qs
                if qtile == 0:
                    wt(SP, "act", a0q_act_end)
                    wt(SP, "dve", a0q_dve_end)
                PROJB = [PS_A, PS_B, PS_G]
                PROJN = ["A", "B", "G"]
                proj_pp = [0] * 4

                def proj_mm(cs):
                    wbuf = cs % 2
                    wt(GP, "pe", wp_used[wbuf])
                    if qtile == 0 and cs < 2:
                        wt(GP, "d_scrw0", 16 * 16)
                        wt(GP, "d_scrw1", 16 * 16)
                    GP.dma_start(WPB[wbuf][:], wproj_bf_dram.ap()[cs].rearrange(
                        "p f c -> p (f c)")).then_inc(sems["d_wp%d" % wbuf], 16)
                    cnt["d_wp%d" % wbuf] += 16
                    wt(PE, "d_wp%d" % wbuf, cnt["d_wp%d" % wbuf])
                    wt(PE, "dve", yT_ready)
                    wbank(PE, PROJN[cs % 3])
                    for ft in range(NH):
                        li = PE.matmul(
                            PROJB[cs % 3][:, :],
                            yT_all[:, ft * 512 + qs * 128: ft * 512 + (qs + 1) * 128],
                            WPB[wbuf][:, ft * 512:(ft + 1) * 512],
                            start=(ft == 0), stop=(ft == NH - 1))
                    p_p = inc(li, "pe")
                    wp_used[wbuf] = p_p
                    proj_pp[cs] = p_p

                def proj_evac(cs):
                    wt(DVE, "pe", proj_pp[cs])
                    wt(DVE, "dve", v_bpb)
                    wt(DVE, "d_scr_r1", scr_r1)
                    DVE.tensor_tensor(
                        r1_tile[:, cs * 512:(cs + 1) * 512], PROJB[cs % 3][:, :],
                        bpb_bf[:, cs * 512:(cs + 1) * 512], ALU.add)
                    bank[PROJN[cs % 3]]["dve"] = cnt["dve"]

                for cs in range(3):
                    proj_mm(cs)
                while pending_xnt:
                    pending_xnt.pop(0)()
                while pending_gate:
                    pending_gate.pop(0)()
                proj_evac(0)
                proj_mm(3)
                wt(SP, "dve", r1_x_add)
                SP.dma_start(xq_sb[:], xo_t[qtile]).then_inc(sems["d_xq"], 16)
                cnt["d_xq"] += 16
                for cs in range(1, 4):
                    proj_evac(cs)
                wt(DVE, "d_xq", cnt["d_xq"])
                DVE.tensor_tensor(r1_tile[:], r1_tile[:], xq_sb[:],
                                                 ALU.add)
                r1_x_add = cnt["dve"]
                DVE.reduce_sum(tcol[:], r1_tile[:], axis=AX.X)
                DVE.tensor_single_scalar(mu[:], tcol[:], 1.0 / 2048,
                                                     ALU.mult)
                v_mu1 = cnt["dve"]
                wt(ACTE, "dve", r1_x_add)
                wt(ACTE, "pe", gate_pe)
                wt(ACTE, "d_scr_xn", scr_xn)
                ACTE.activation(xn1_st[:], r1_tile[:], AF.Square,
                                accum_out=ssq[:])
                a_s1 = cnt["act"]
                wt(DVE, "act", a_s1)
                DVE.tensor_single_scalar(t2col[:], ssq[:], 1.0 / 2048, ALU.mult)
                DVE.tensor_tensor(tcol[:], mu[:], mu[:], ALU.mult)
                DVE.tensor_tensor(vvar[:], t2col[:], tcol[:],
                                             ALU.subtract)
                v_v1 = cnt["dve"]
                wt(ACTE, "dve", v_v1)
                ACTE.activation(lncol[:], vvar[:], AF.Ln, bias=c_eps[:])
                ACTE.activation(sinv[:], lncol[:], AF.Exp, scale=c_nh[:])
                a_r1 = cnt["act"]
                wt(DVE, "act", a_r1)
                wt(DVE, "dve", v_mu1)
                wt(DVE, "pe", xn_pe_last)
                DVE.tensor_scalar(xn1_nat[:], r1_tile[:], mu[:], sinv[:],
                                             ALU.subtract, ALU.mult)
                v_xn = cnt["dve"]

                def _xnt(qtile=qtile, v_xn=v_xn):
                    nonlocal xn_pe_last, xn_tr_done, scr_xn
                    for hh in range(2):
                        wt(PE, "dve", v_xn)
                        wt(PE, "dve", xn_ev[hh])
                        for k in range(8):
                            ctk = hh * 8 + k
                            li = PE.transpose(
                                PS_E[:, hh * 1024 + k * 128: hh * 1024 + (k + 1) * 128],
                                xn1_nat[:, ctk * 128:(ctk + 1) * 128], ident[:])
                        p_x = inc(li, "pe")
                        xn_pe_last = p_x
                        wt(DVE, "pe", p_x)
                        wt(DVE, "d_scr_xn", scr_xn)
                        wt(DVE, "pe", gate_pe)
                        DVE.tensor_copy(
                            xn1_st[:, hh * 1024:(hh + 1) * 1024],
                            PS_E[:, hh * 1024:(hh + 1) * 1024])
                        xn_ev[hh] = cnt["dve"]
                    xn_tr_done = max(xn_ev)
                    wt(SP, "dve", xn_tr_done)
                    SP.dma_start(xn1t_dram.ap()[qtile], xn1_st[:]
                                 ).then_inc(sems["d_scr_xn"], 16)
                    cnt["d_scr_xn"] += 16
                    scr_xn = cnt["d_scr_xn"]
                pending_xnt.append(_xnt)
                def _gate_mm(qtile=qtile):
                    nonlocal gate_pe
                    wt(PE, "dve", xn_tr_done)
                    wt(PE, "act", den_free)
                    wt(PE, "act", bankF_act[0])
                    wt(PE, "d_setupg", setup_gp)
                    for ct in range(NCT):
                        li = PE.matmul(PS_F[:, 0:32],
                                       xn1_st[:, ct * 128:(ct + 1) * 128],
                                       gate_bf[:, ct * 32:(ct + 1) * 32],
                                       start=(ct == 0), stop=(ct == NCT - 1))
                    p_lg = inc(li, "pe")
                    gate_pe = p_lg
                    wt(ACTE, "pe", p_lg)
                    wt(ACTE, "dve", gate_dve_prev[0])
                    ACTE.activation(probs[:], PS_F[:, 0:32], AF.Exp,
                                    accum_out=gden[:])
                    a_g = cnt["act"]
                    bankF_act[0] = a_g

                    def _gate_dve(qtile=qtile, a_g=a_g):
                        wt(DVE, "act", a_g)
                        DVE.reciprocal(gdeni[:], gden[:])
                        DVE.tensor_scalar(probs[:], probs[:], gdeni[:], None, ALU.mult)
                        DVE.tensor_copy(gwork[:], probs[:])
                        for i8 in range(8):
                            DVE.reduce_max(gmc[:], gwork[:], axis=AX.X)
                            DVE.tensor_copy(gm8[:, i8:i8 + 1], gmc[:])
                            if i8 < 7:
                                DVE.tensor_scalar(gtmp[:], gwork[:], gmc[:], NEG,
                                                  ALU.is_ge, ALU.mult)
                                DVE.tensor_tensor(gwork[:], gwork[:], gtmp[:], ALU.add)
                        DVE.tensor_copy(gcs[:, 0:1], gm8[:, 0:1])
                        for i8 in range(1, 8):
                            DVE.tensor_tensor(gcs[:, i8:i8 + 1], gcs[:, i8 - 1:i8],
                                              gm8[:, i8:i8 + 1], ALU.add)
                        DVE.tensor_single_scalar(gthr[:], gcs[:, 7:8], 0.8, ALU.mult)
                        DVE.memset(gkeep[:, 0:1], 1.0)
                        DVE.tensor_scalar(gkeep[:, 1:8], gcs[:, 0:7], gthr[:], None,
                                          ALU.is_lt)
                        DVE.tensor_tensor(gtmp[:, 0:8], gm8[:], gkeep[:], ALU.mult)
                        DVE.reduce_sum(gSk[:], gtmp[:, 0:8], axis=AX.X)
                        DVE.tensor_scalar(gwork[:, 0:8], gkeep[:], 1.0, NEG,
                                          ALU.subtract, ALU.mult)
                        DVE.tensor_tensor(gwork[:, 0:8], gwork[:, 0:8], gtmp[:, 0:8],
                                          ALU.add)
                        DVE.tensor_reduce(gtau[:], gwork[:, 0:8], axis=AX.X, op=ALU.min)
                        DVE.reciprocal(gSki[:], gSk[:])
                        DVE.tensor_scalar(gtmp[:], probs[:], gtau[:], None, ALU.is_ge)
                        DVE.tensor_tensor(gtmp[:], gtmp[:], probs[:], ALU.mult)
                        DVE.tensor_scalar(comb[:, qtile * 32:(qtile + 1) * 32], gtmp[:],
                                          gSki[:], None, ALU.mult)
                        gate_dve_prev[0] = cnt["dve"]
                    pending_gdve.append(_gate_dve)
                pending_gate.append(_gate_mm)
                wt(SP, "dve", r1_x_add)
                SP.dma_start(r1_dram.ap()[qtile * 128:(qtile + 1) * 128, :],
                             r1_tile[:]).then_inc(sems["d_scr_r1"], 16)
                cnt["d_scr_r1"] += 16
                scr_r1 = cnt["d_scr_r1"]
                while pending_gdve:
                    pending_gdve.pop(0)()

        while pending_xnt:
            pending_xnt.pop(0)()
        while pending_gate:
            pending_gate.pop(0)()
        while pending_gdve:
            pending_gdve.pop(0)()
        if debug:
            wt(SP, "dve", v_kc)
            SP.dma_start(dbg_kt.ap(), kc_sb[:]).then_inc(sems["d_out"], 16)
            cnt["d_out"] += 16
            wt(SP, "dve", knat_done)
            SP.dma_start(dbg_knat.ap(), knat[:]).then_inc(sems["d_out"], 16)
            cnt["d_out"] += 16
            wt(SP, "act", q_done)
            SP.dma_start(dbg_qt.ap(), qT[:]).then_inc(sems["d_out"], 16)
            cnt["d_out"] += 16

        if upto == "amain":
            wt(SP, "d_scr_r1", cnt["d_scr_r1"])
            wt(SP, "d_scr_xn", cnt["d_scr_xn"])
            wt(SP, "d_out", cnt["d_out"])
            return nc
        a_end_pe = cnt["pe"]
        a_end_act = cnt["act"]
        a_end_dve = cnt["dve"]
        a_end_r1 = cnt["d_scr_r1"]
        a_end_xn = cnt["d_scr_xn"]
        for bn in bank:
            bank[bn] = {"act": 0, "dve": 0}

        WGU = [wgu_bf0, wgu_bf1, wgu_bf2]
        HT = [hT0, hT1]
        R1RD = [r1rd0, r1rd1]
        wgu_used = {}
        wdn_used = a_end_pe
        gsil_guard = a_end_dve
        out_add = [a_end_dve, a_end_dve]
        out_st_used = [0, 0]
        xn1_pe_used = a_end_pe
        out_dma_n = 0
        gb = {k: {"act": a_end_act, "dve": a_end_dve} for k in ["A", "B", "C", "D", "F"]}

        for hf in range(2):
            if hf == 0:
                wt(SP, "d_scr_r1", a_end_r1)
                wt(SP, "d_scr_xn", a_end_xn)
                wt(SP, "d_out", cnt["d_out"])
                wt(SP, "d_xq", cnt["d_xq"])
                wt(SP, "pe", a_end_pe)
                wt(SP, "act", a_end_act)
                wt(SP, "dve", a_end_dve)
                wt(GP, "d_scr_r1", a_end_r1)
                wt(GP, "d_scr_xn", a_end_xn)
                wt(GP, "d_xq", cnt["d_xq"])
                wt(GP, "d_out", cnt["d_out"])
                wt(GP, "pe", a_end_pe)
                wt(GP, "act", a_end_act)
                wt(GP, "dve", a_end_dve)
            else:
                wt(SP, "pe", xn1_pe_used)
            xsrc = xn1t_dram.ap().rearrange("q p f -> p q f")
            for ct in range(NCT):
                SP.dma_start(
                    xn1T_sb[:, ct * 1024:(ct + 1) * 1024].rearrange(
                        "p (q t) -> p q t", t=128),
                    xsrc[:, 8 * hf:8 * hf + 8, ct * 128:(ct + 1) * 128]
                ).then_inc(sems["d_xn1t"], 16)
                cnt["d_xn1t"] += 16
            xn1_ready = cnt["d_xn1t"]
            if hf == 0:
                wt(DVE, "pe", a_end_pe)
                wt(DVE, "act", a_end_act)
                wt(DVE, "d_out", cnt["d_out"])
                wt(DVE, "d_scr_r1", a_end_r1)
                wt(DVE, "d_scr_xn", a_end_xn)
                wt(DVE, "d_xq", cnt["d_xq"])
            else:
                wt(DVE, "d_out0", cnt["d_out0"])
                wt(DVE, "d_out1", cnt["d_out1"])
                wt(DVE, "d_out", cnt["d_out"])
            DVE.engine_nop()
            v_acc0 = cnt["dve"]

            if debug and hf == 0:
                SP.dma_start(dbg_comb.ap(), comb[:]).then_inc(sems["d_out"], 16)
                cnt["d_out"] += 16

            for e in range(NE):
                ebuf = e % 2
                for jc in range(12):
                    if 6 <= jc <= 9:
                        hcg = jc - 6
                        if hcg == 0:
                            wt(GP, "pe", wdn_used)
                        GP.dma_start(
                            wdn_bf[:, hcg * 3072:(hcg + 1) * 3072].rearrange(
                                "p (i h) -> p i h", h=512),
                            wdn_in.ap()[e, hcg].rearrange("i p h -> p i h")
                        ).then_inc(sems["d_wdn"], 16)
                        cnt["d_wdn"] += 16
                        wdn_ready = cnt["d_wdn"]
                    gjc = (hf * NE + e) * 12 + jc
                    wbuf = gjc % 3
                    wt(GP, "pe", wgu_used.get(gjc - 3, 0))
                    GP.dma_start(WGU[wbuf][:], wgu_in.ap()[e, jc].rearrange(
                        "p c j -> p (c j)")).then_inc(sems["d_wgu%d" % wbuf], 16)
                    cnt["d_wgu%d" % wbuf] += 16
                    wgu_dma = cnt["d_wgu%d" % wbuf]
                    for ts in range(2):
                        pbn = ABN[(jc * 2 + ts) % 2]
                        pb = PS_AB[(jc * 2 + ts) % 2]
                        wt(PE, "d_wgu%d" % wbuf, wgu_dma)
                        wt(PE, "d_xn1t", xn1_ready)
                        wt(PE, "act", gb[pbn]["act"])
                        wt(PE, "dve", gb[pbn]["dve"])
                        for ct in range(NCT):
                            li = PE.matmul(
                                pb[:, :], WGU[wbuf][:, ct * 128:(ct + 1) * 128],
                                xn1T_sb[:, ct * 1024 + ts * 512: ct * 1024 + (ts + 1) * 512],
                                start=(ct == 0), stop=(ct == NCT - 1))
                        p_gu = inc(li, "pe")
                        if jc < 6:
                            wt(ACTE, "pe", p_gu)
                            if jc == 0 and ts == 0:
                                wt(ACTE, "dve", gsil_guard)
                            ACTE.activation(
                                g_sil[:, jc * 1024 + ts * 512: jc * 1024 + (ts + 1) * 512],
                                pb[:, :], AF.Silu)
                            a_si = cnt["act"]
                            gb[pbn]["act"] = a_si
                        else:
                            ic = jc - 6
                            wt(DVE, "pe", p_gu)
                            wt(DVE, "act", sil_done)
                            DVE.tensor_tensor(
                                HT[ebuf][:, ic * 1024 + ts * 512: ic * 1024 + (ts + 1) * 512],
                                g_sil[:, ic * 1024 + ts * 512: ic * 1024 + (ts + 1) * 512],
                                pb[:, :], ALU.mult)
                            gb[pbn]["dve"] = cnt["dve"]
                    wgu_used[gjc] = cnt["pe"]
                    if jc == 5:
                        sil_done = cnt["act"]
                hT_done = cnt["dve"]
                gu_end_pe = cnt["pe"]
                gsil_guard = hT_done
                for tt in range(8):
                    for hcg in range(4):
                        pbn = CDN[(tt * 4 + hcg) % 2]
                        pb = PS_CD[(tt * 4 + hcg) % 2]
                        wt(PE, "dve", hT_done)
                        wt(PE, "d_wdn", wdn_ready)
                        wt(PE, "dve", gb[pbn]["dve"])
                        for it in range(6):
                            li = PE.matmul(
                                pb[:, :],
                                HT[ebuf][:, it * 1024 + tt * 128: it * 1024 + (tt + 1) * 128],
                                wdn_bf[:, hcg * 3072 + it * 512: hcg * 3072 + (it + 1) * 512],
                                start=(it == 0), stop=(it == 5))
                        p_dn = inc(li, "pe")
                        wt(DVE, "pe", p_dn)
                        wt(DVE, "dve", v_acc0)
                        ccol = comb[:, (hf * 8 + tt) * 32 + e:
                                    (hf * 8 + tt) * 32 + e + 1]
                        aslc = acc[:, tt * C + hcg * 512: tt * C + (hcg + 1) * 512]
                        if e == 0:
                            DVE.tensor_scalar(aslc, pb[:, :], ccol, None, ALU.mult)
                        else:
                            DVE.scalar_tensor_tensor(aslc, pb[:, :], ccol, aslc,
                                                     ALU.mult, ALU.add)
                        gb[pbn]["dve"] = cnt["dve"]
                wdn_used = cnt["pe"]
                if e == NE - 1:
                    xn1_pe_used = gu_end_pe
            ffn_done = cnt["dve"]

            for tt in range(8):
                qtile = hf * 8 + tt
                rb = tt % 2
                wt(SP, "d_scr_r1", a_end_r1)
                wt(SP, "d_out%d" % rb, cnt["d_out%d" % rb])
                SP.dma_start(R1RD[rb][:],
                             r1_dram.ap()[qtile * 128:(qtile + 1) * 128, :]
                             ).then_inc(sems["d_r1%d" % rb], 16)
                cnt["d_r1%d" % rb] += 16
                wt(DVE, "d_r1%d" % rb, cnt["d_r1%d" % rb])
                wt(DVE, "dve", ffn_done)
                DVE.tensor_tensor(
                    R1RD[rb][:], acc[:, tt * C:(tt + 1) * C], R1RD[rb][:],
                    ALU.add)
                out_add[rb] = cnt["dve"]
                wt(SP, "dve", out_add[rb])
                SP.dma_start(out_own.ap()[qtile * 128:(qtile + 1) * 128, :],
                             R1RD[rb][:]).then_inc(sems["d_out%d" % rb], 16)
                cnt["d_out%d" % rb] += 16
            out_dma_n = cnt["d_out0"] + cnt["d_out1"]

        wt(SP, "d_out", cnt["d_out"])
        wt(SP, "d_out0", cnt["d_out0"])
        wt(SP, "d_out1", cnt["d_out1"])

    return nc


def _prep_shared(inputs):
    f32 = np.float32
    W_lat = np.asarray(inputs["L0_attn_W_lat"], f32)
    W_d = np.asarray(inputs["L0_attn_W_d"], f32)
    W_proj = np.asarray(inputs["L0_attn_W_proj"], f32)
    gate_w = np.asarray(inputs["L0_ffn_gate_weight"], f32)
    W_gu = np.asarray(inputs["L0_ffn_gate_up_proj"], f32)
    W_dn = np.asarray(inputs["L0_ffn_down_proj"], f32)
    d = {}
    d["wlat_in"] = np.ascontiguousarray(
        W_lat.reshape(16, 128, 128).transpose(1, 0, 2))
    d["wd_in"] = np.ascontiguousarray(
        W_d.reshape(16, 128, 16, 128).transpose(2, 1, 0, 3))
    d["wproj_in"] = np.ascontiguousarray(
        W_proj.reshape(16, 128, 4, 512).transpose(2, 1, 0, 3))
    d["gate_in"] = np.ascontiguousarray(
        gate_w.T.reshape(16, 128, 32).transpose(1, 0, 2))
    d["wgu_in"] = np.ascontiguousarray(
        W_gu.reshape(32, 12, 128, 16, 128).transpose(0, 1, 4, 3, 2))
    d["wdn_in"] = np.ascontiguousarray(
        W_dn.reshape(32, 4, 512, 6, 128).transpose(0, 1, 3, 4, 2))
    d["blat_in"] = np.asarray(inputs["L0_attn_b_lat"], f32)
    d["bd_in"] = np.asarray(inputs["L0_attn_b_d"], f32)
    d["bproj_in"] = np.asarray(inputs["L0_attn_b_proj"], f32)
    d["ident_in"] = np.eye(128, dtype=BFnp)
    d["identf_in"] = np.eye(128, dtype=np.float32)
    return d


def _masks_for(j):
    p = np.arange(128)[:, None]
    f = np.arange(512)[None, :]
    out = np.empty((16, 128, 512), dtype=BFnp)
    for dd in range(16):
        out[dd] = np.where(4 * f + j >= 128 * dd + p, np.float32(0.0),
                           np.float32(-1e30)).astype(BFnp)
    return out


def kernel(**inputs):
    x = np.asarray(inputs["L0_attn_x"], np.float32)
    shared = _prep_shared(inputs)
    nc = build(debug=False)
    in_maps = []
    for c in range(8):
        b, j = c // 4, c % 4
        m = dict(shared)
        m["x_full"] = np.ascontiguousarray(x[b])
        m["x_own"] = np.ascontiguousarray(x[b, j::4])
        m["masks_in"] = _masks_for(j)
        in_maps.append(m)
    res = run_bass_kernel_spmd(nc, in_maps, core_ids=list(range(8)))
    out = np.empty((B, T, C), dtype=np.float32)
    for c in range(8):
        b, j = c // 4, c % 4
        out[b, j::4] = np.asarray(res.results[c]["out_own"], np.float32)
    return out
